# version 23
# baseline (speedup 1.0000x reference)
import contextlib

import numpy as np
import ml_dtypes

import concourse.bass as bass
import concourse.mybir as mybir
from concourse.bass_utils import run_bass_kernel_spmd

F32 = mybir.dt.float32
BF16 = mybir.dt.bfloat16
AF = mybir.ActivationFunctionType
ALU = mybir.AluOpType
AX = mybir.AxisListType.X

NCORE = 8
D = 4096
S = 8192
TL = S // NCORE
NT = TL // 128
KC = D // 128
ECH = D // 128
HEADS = 16
HD = 256
ROT = 64
NE = 8
NR = 16
EPS = 1e-5

ENGS = ("sync", "tensor", "vector", "scalar", "gpsimd")


class Em:

    def __init__(self):
        self.ops = {e: [] for e in ENGS}
        self.cnt = {}
        self.waited = {e: {} for e in ENGS}
        self.H = {}

    def sems(self, *names):
        for n in names:
            self.cnt[n] = 0

    def op(self, eng, fn, inc=None, n=1, fin=False):
        if inc is not None:
            self.cnt[inc] += n
            if (eng in ("vector", "scalar") or fin) and n == 1:
                def g(E, f=fn, s=inc, k=n):
                    f(E)
                    E.maybe_drain_then_inc((self.H[s], k))
                self.ops[eng].append(g)
            else:
                self.ops[eng].append(
                    lambda E, f=fn, s=inc, k=n: f(E).then_inc(self.H[s], k))
            return self.cnt[inc]
        self.ops[eng].append(fn)
        return None

    def wait(self, eng, name, val=None):
        v = self.cnt[name] if val is None else val
        if v <= 0 or v <= self.waited[eng].get(name, 0):
            return
        self.waited[eng][name] = v
        self.ops[eng].append(lambda E, s=name, w=v: E.wait_ge(self.H[s], w))


def build_program():
    nc = bass.Bass("TRN2", target_bir_lowering=False, dynamic_dma_scratch_size=32768)

    x_in = nc.dram_tensor("x", [TL, D], F32, kind="ExternalInput").ap()
    wq_in = nc.dram_tensor("wq", [D, D], F32, kind="ExternalInput").ap()
    wk_in = nc.dram_tensor("wk", [D, D], F32, kind="ExternalInput").ap()
    wv_in = nc.dram_tensor("wv", [D, D], F32, kind="ExternalInput").ap()
    wo_in = nc.dram_tensor("wo", [D, D], F32, kind="ExternalInput").ap()
    wbt_in = nc.dram_tensor("wbt", [D, D], F32, kind="ExternalInput").ap()
    bb_in = nc.dram_tensor("bb", [1, D], F32, kind="ExternalInput").ap()
    wrt_in = nc.dram_tensor("wrt_h", [D, NE], F32, kind="ExternalInput").ap()
    amt_in = nc.dram_tensor("amt", [D, 128], F32, kind="ExternalInput").ap()
    bmr_in = nc.dram_tensor("bmr", [128, D], F32, kind="ExternalInput").ap()
    ident_in = nc.dram_tensor("ident", [128, 128], BF16, kind="ExternalInput").ap()
    masks_in = nc.dram_tensor("masks", [128, 4 * 512], BF16, kind="ExternalInput").ap()
    indm_in = nc.dram_tensor("indm", [NE, 128], BF16, kind="ExternalInput").ap()
    cexp_in = nc.dram_tensor("cexp", [ROT, TL], F32, kind="ExternalInput").ap()
    sexp_in = nc.dram_tensor("sexp", [ROT, TL], F32, kind="ExternalInput").ap()
    y_out = nc.dram_tensor("y", [TL, D], F32, kind="ExternalOutput").ap()

    qt_loc = nc.dram_tensor("qt_loc", [ECH, 128, TL], BF16, kind="Internal").ap()
    kt_loc = nc.dram_tensor("kt_loc", [ECH, 128, TL], BF16, kind="Internal").ap()
    v_loc = nc.dram_tensor("v_loc", [NT, 128, D], BF16, kind="Internal").ap()
    qt_all = nc.dram_tensor("qt_all", [NCORE, ECH, 128, TL], BF16, kind="Internal",
                            addr_space="Shared").ap()
    kt_all = nc.dram_tensor("kt_all", [NCORE, ECH, 128, TL], BF16, kind="Internal",
                            addr_space="Shared").ap()
    v_all = nc.dram_tensor("v_all", [NCORE, NT, 128, D], BF16, kind="Internal",
                           addr_space="Shared").ap()
    aot_loc = nc.dram_tensor("aot_loc", [4, 128, S], BF16, kind="Internal").ap()
    aot_all = nc.dram_tensor("aot_all", [NCORE, 4, 128, S], BF16, kind="Internal",
                             addr_space="Shared").ap()
    kt_my = nc.dram_tensor("kt_my", [4, 128, S], BF16, kind="Internal").ap()
    qt_my = nc.dram_tensor("qt_my", [4, 128, S], BF16, kind="Internal").ap()
    v_my = nc.dram_tensor("v_my", [NCORE, NT, 128, 512], BF16, kind="Internal").ap()
    aot_my = nc.dram_tensor("aot_my", [NCORE, 4, 128, TL], BF16, kind="Internal").ap()
    wbt16 = nc.dram_tensor("wbt16", [D, D], BF16, kind="Internal").ap()
    wo16 = nc.dram_tensor("wo16", [D, D], BF16, kind="Internal").ap()
    RG = [list(range(NCORE))]

    em = Em()
    em.sems("sS", "sG", "sW", "sT", "sV", "sA", "sC", "sCq", "sCk")
    PID = {}

    es = contextlib.ExitStack()
    with es:
        sb = lambda name, shape, dt: es.enter_context(nc.sbuf_tensor(name, shape, dt))
        ident = sb("identt", [128, 128], BF16)
        indm = sb("indmt", [NE, 128], BF16)
        at16 = sb("at16", [128, KC * 128], BF16)
        b3 = sb("b3", [128, KC * 128], BF16)
        wrt = sb("wrt", [128, KC * NE], BF16)
        bb16 = sb("bb16", [1, D], BF16)
        onesr = sb("onesr", [1, 128], BF16)
        stat = sb("stat", [128, 8, 6], F32)
        mvv = sb("mvv", [128, 2], F32)
        sdv = sb("sdv", [128, 1], F32)
        rsv = sb("rsv", [128, 1], F32)
        r8a = sb("r8a", [128, 8], F32)
        r8b = sb("r8b", [128, 8], F32)
        r8c = sb("r8c", [128, 8], F32)
        m1v = sb("m1v", [128, 1], F32)
        m2v = sb("m2v", [128, 1], F32)
        g2v = sb("g2v", [128, 1], F32)
        g1v = sb("g1v", [128, 1], F32)
        w8b = sb("w8b", [128, 8], BF16)
        gt = sb("gtb", [128, 512], BF16)
        wrepb = sb("wrepb", [128, 512], F32)
        epsv = sb("epsv", [128, 1], F32)

        psum = es.enter_context(nc.psum_tensor("ps", [128, 4096], F32))
        PS = lambda bank, w=512, off=0: psum[:, bank * 512 + off: bank * 512 + off + w]
        PSB = lambda bank, wb=128, offb=0: psum[:, bank * 512 + offb // 2:
                                                bank * 512 + offb // 2 + wb // 2].bitcast(BF16)

        for s in ("sS", "sG", "sW", "sT", "sV", "sA", "sC", "sCq", "sCk"):
            em.H[s] = es.enter_context(nc.semaphore(s))

        em.op("sync", lambda E: E.dma_start(out=ident[:, :], in_=ident_in), inc="sS", n=16)
        em.op("sync", lambda E: E.dma_start(out=indm[:, :], in_=indm_in), inc="sS", n=16)

        em.op("gpsimd", lambda E: E.dma_start(out=bb16[:, :], in_=bb_in), inc="sG", n=16)
        for i in range(4):
            em.op("gpsimd", lambda E, i=i: E.dma_start(
                out=wbt16[i * 1024:(i + 1) * 1024, :], in_=wbt_in[i * 1024:(i + 1) * 1024, :]),
                inc="sG", n=16)
            em.op("gpsimd", lambda E, i=i: E.dma_start(
                out=wo16[i * 1024:(i + 1) * 1024, :], in_=wo_in[i * 1024:(i + 1) * 1024, :]),
                inc="sG", n=16)

        em.op("vector", lambda E: E.memset(onesr[:, :], 1.0), inc="sV")
        em.op("vector", lambda E: E.memset(epsv[:, :], EPS), inc="sV")

        with nc.sbuf_tensor("xlnT", [128, KC * TL], BF16) as xlnT, \
             nc.sbuf_tensor("cexpt", [ROT, TL], F32) as cexp, \
             nc.sbuf_tensor("sexpt", [ROT, TL], F32) as sexp:
            em.op("sync", lambda E: E.dma_start(out=cexp[:, :], in_=cexp_in), inc="sS", n=16)
            em.op("sync", lambda E: E.dma_start(out=sexp[:, :], in_=sexp_in), inc="sS", n=16)
            with nc.sbuf_tensor("xbuf", [128, 2, D], F32) as xbuf, \
                 nc.sbuf_tensor("xln", [128, D], BF16) as xln:
                with nc.sbuf_tensor("prep32", [128, KC * 128], F32) as prep32:
                    em.op("sync", lambda E: E.dma_start(out=prep32[:, :], in_=bmr_in), inc="sS", n=16)
                    em.wait("vector", "sS")
                    em.op("vector", lambda E: E.tensor_scalar(
                        b3[:, :], prep32[:, :], 2.0, None, ALU.mult), inc="sV")
                    b3_done = em.cnt["sV"]
                    em.wait("sync", "sV", b3_done)
                    em.op("sync", lambda E: E.dma_start(
                        out=prep32.rearrange("p (kc j) -> p kc j", j=128),
                        in_=amt_in.rearrange("(kc p) j -> p kc j", p=128)), inc="sS", n=16)
                    em.wait("vector", "sS")
                    em.op("vector", lambda E: E.tensor_copy(at16[:, :], prep32[:, :]), inc="sV")
                    at_done = em.cnt["sV"]
                    em.wait("sync", "sV", at_done)
                    em.op("sync", lambda E: E.dma_start(
                        out=prep32.rearrange("p (kc e) -> p kc e", e=NE)[:, 0:KC, :],
                        in_=wrt_in.rearrange("(kc p) e -> p kc e", p=128)), inc="sS", n=16)
                    em.wait("vector", "sS")
                    em.op("vector", lambda E: E.tensor_copy(wrt[:, :], prep32[:, 0:KC * NE]), inc="sV")


                for tt in range(NT):
                    par = tt % 2
                    if tt >= 2:
                        em.wait("sync", "sA")
                    em.op("sync", lambda E, tt=tt, p=par: E.dma_start(
                        out=xbuf[:, p, :], in_=x_in[tt * 128:(tt + 1) * 128, :]), inc="sS", n=16)
                    em.wait("vector", "sS")
                    for c in range(8):
                        em.op("vector", lambda E, p=par, c=c: E.bn_stats(
                            stat[:, c, :], xbuf[:, p, 512 * c:512 * (c + 1)]), inc="sV")
                    em.op("vector", lambda E: E.bn_aggr(mvv[:, :], stat[:, :, :]), inc="sV")
                    em.wait("scalar", "sV")
                    em.op("scalar", lambda E: E.activation(out=sdv[:, :], in_=mvv[:, 1:2], func=AF.Sqrt,
                                                           bias=epsv[:, :]), inc="sA")
                    em.wait("vector", "sA")
                    em.op("vector", lambda E: E.reciprocal(rsv[:, :], sdv[:, :]), inc="sV")
                    em.op("vector", lambda E, p=par: E.tensor_scalar(
                        xln[:, :], xbuf[:, p, :], mvv[:, 0:1], rsv[:, :], ALU.subtract, ALU.mult), inc="sV")
                    em.wait("scalar", "sV")
                    em.op("scalar", lambda E: E.activation(out=sdv[:, :], in_=sdv[:, :], func=AF.Copy), inc="sA")
                    em.wait("tensor", "sV")
                    for kg in range(8):
                        quad = (kg % 2) * 4
                        em.wait("tensor", "sV")
                        for j in range(4):
                            kc = kg * 4 + j
                            em.op("tensor", lambda E, kc=kc, b=quad + j: E.transpose(
                                PSB(b, 128), xln[:, kc * 128:(kc + 1) * 128], ident[:, :]), inc="sT",
                                fin=(j == 3))
                        em.wait("vector", "sT")
                        for j in range(4):
                            kc = kg * 4 + j
                            dst = xlnT.rearrange("p (kc t) -> p kc t", t=TL)[
                                :, kc, tt * 128:(tt + 1) * 128]
                            em.op("vector", lambda E, b=quad + j, d=dst: E.tensor_copy(
                                d, PSB(b, 128)), inc="sV")

            def qk_phase(w_src, dst_loc, tag):
                with nc.sbuf_tensor("wsl" + tag, [128, 2, KC * 256], BF16) as wsl, \
                     nc.sbuf_tensor("wslf" + tag, [128, 2, KC * 128], F32) as wslf, \
                     nc.sbuf_tensor("stgf" + tag, [128, TL], F32) as stgf, \
                     nc.sbuf_tensor("stg16" + tag, [128, TL], BF16) as stg16, \
                     nc.sbuf_tensor("rot1" + tag, [ROT, 512], F32) as rot1, \
                     nc.sbuf_tensor("rot2" + tag, [ROT, 512], F32) as rot2, \
                     nc.sbuf_tensor("rotsw" + tag, [ROT, 512], F32) as rotsw:
                    slab_T = {}
                    cast_V = {}
                    em.wait("sync", "sT")
                    em.wait("sync", "sV")
                    em.wait("sync", "sA")
                    em.wait("sync", "sW")

                    def emit_half(s_, et):
                        hpar = (s_ * 2 + et) % 2
                        if (s_, et) >= (1, 0):
                            prev = (s_ * 2 + et) - 2
                            if prev >= 0:
                                em.wait("sync", "sV", cast_V.get(prev, 0))
                        src = w_src[:, s_ * 256 + et * 128: s_ * 256 + (et + 1) * 128].rearrange(
                            "(kc p) e -> p kc e", p=128)
                        em.op("sync", lambda E, src=src, hp=hpar: E.dma_start(
                            out=wslf.rearrange("p b (kc e) -> p b kc e", e=128)[:, hp, :, :],
                            in_=src), inc="sS", n=16)
                        ld = em.cnt["sS"]
                        em.wait("vector", "sS", ld)
                        if s_ >= 2:
                            em.wait("vector", "sT", slab_T[s_ - 2])
                        dst = wsl.rearrange("p b (kc e) -> p b kc e", e=256)[
                            :, s_ % 2, :, et * 128:(et + 1) * 128]
                        em.op("vector", lambda E, hp=hpar, d=dst: E.tensor_copy(
                            d, wslf.rearrange("p b (kc e) -> p b kc e", e=128)[:, hp, :, :]),
                            inc="sV")
                        cast_V[s_ * 2 + et] = em.cnt["sV"]

                    emit_half(0, 0)
                    emit_half(0, 1)
                    for s_ in range(16):
                        par = s_ % 2
                        if s_ + 1 < 16:
                            emit_half(s_ + 1, 0)
                            emit_half(s_ + 1, 1)
                        for et in range(2):
                            e_g = s_ * 2 + et
                            em.wait("tensor", "sV", cast_V[s_ * 2 + et])
                            for ch in range(2):
                                bank = et * 2 + ch
                                em.wait("tensor", "sA")
                                for kc in range(KC):
                                    em.op("tensor",
                                          lambda E, kc=kc, b=bank, p=par, et=et, ch=ch: E.matmul(
                                              PS(b),
                                              wsl[:, p, kc * 256 + et * 128: kc * 256 + (et + 1) * 128],
                                              xlnT[:, kc * TL + ch * 512: kc * TL + (ch + 1) * 512],
                                              start=(kc == 0), stop=(kc == KC - 1)), inc="sT",
                                          fin=(kc == KC - 1))
                            em.wait("scalar", "sT")
                            em.wait("scalar", "sV")
                            for ch in range(2):
                                bank = et * 2 + ch
                                em.op("scalar", lambda E, b=bank, ch=ch: E.activation(
                                    out=stgf[:, ch * 512:(ch + 1) * 512], in_=PS(b), func=AF.Copy),
                                    inc="sA")
                            rot = (e_g % 2 == 0)
                            if rot:
                                em.wait("vector", "sA")
                                for ch in range(2):
                                    sl = slice(ch * 512, (ch + 1) * 512)
                                    em.op("vector", lambda E, sl=sl: E.stream_shuffle(
                                        rotsw[:, :], stgf[0:ROT, sl], [p ^ 1 for p in range(32)]), inc="sV")
                                    em.op("vector", lambda E, sl=sl: E.tensor_mul(
                                        rot1[:, :], stgf[0:ROT, sl], cexp[:, sl]), inc="sV")
                                    em.op("vector", lambda E, sl=sl: E.tensor_mul(
                                        rot2[:, :], rotsw[:, :], sexp[:, sl]), inc="sV")
                                    em.op("vector", lambda E, sl=sl: E.tensor_add(
                                        stgf[0:ROT, sl], rot1[:, :], rot2[:, :]), inc="sV")
                            em.wait("vector", "sA")
                            em.wait("vector", "sW")
                            em.op("vector", lambda E: E.tensor_copy(stg16[:, :], stgf[:, :]), inc="sV")
                            em.wait("scalar", "sV")
                            em.op("scalar", lambda E, e_g=e_g, d=dst_loc: E.dma_start(
                                out=d[e_g, :, :], in_=stg16[:, :]), inc="sW", n=16)
                        slab_T[s_] = em.cnt["sT"]

            qk_phase(wq_in, qt_loc, "q")
            em.ops["gpsimd"].insert(0, lambda E: PID.__setitem__("g", E.partition_id()))
            qk_phase(wk_in, kt_loc, "k")

            with nc.sbuf_tensor("vsl", [128, 2, KC * 256], BF16) as vsl, \
                 nc.sbuf_tensor("vslf", [128, 2, KC * 128], F32) as vslf, \
                 nc.sbuf_tensor("vstg", [128, 2, 256], BF16) as vstg:
                slab_T = {}
                cast_V = {}
                em.wait("sync", "sT")
                em.wait("sync", "sV")
                em.wait("sync", "sA")
                em.wait("sync", "sW")

                def emit_vhalf(s_, et):
                    hpar = (s_ * 2 + et) % 2
                    prev = (s_ * 2 + et) - 2
                    if prev >= 0:
                        em.wait("sync", "sV", cast_V.get(prev, 0))
                    src = wv_in[:, s_ * 256 + et * 128: s_ * 256 + (et + 1) * 128].rearrange(
                        "(kc p) e -> p kc e", p=128)
                    em.op("sync", lambda E, src=src, hp=hpar: E.dma_start(
                        out=vslf.rearrange("p b (kc e) -> p b kc e", e=128)[:, hp, :, :],
                        in_=src), inc="sS", n=16)
                    ld = em.cnt["sS"]
                    em.wait("vector", "sS", ld)
                    if s_ >= 2:
                        em.wait("vector", "sT", slab_T[s_ - 2])
                    dst = vsl.rearrange("p b (kc e) -> p b kc e", e=256)[
                        :, s_ % 2, :, et * 128:(et + 1) * 128]
                    em.op("vector", lambda E, hp=hpar, d=dst: E.tensor_copy(
                        d, vslf.rearrange("p b (kc e) -> p b kc e", e=128)[:, hp, :, :]),
                        inc="sV")
                    cast_V[s_ * 2 + et] = em.cnt["sV"]

                emit_vhalf(0, 0)
                emit_vhalf(0, 1)
                for s_ in range(16):
                    par = s_ % 2
                    if s_ + 1 < 16:
                        emit_vhalf(s_ + 1, 0)
                        emit_vhalf(s_ + 1, 1)
                    em.wait("tensor", "sV", cast_V[s_ * 2 + 1])
                    em.wait("tensor", "sA")
                    for tt in range(NT):
                        for kc in range(KC):
                            em.op("tensor", lambda E, kc=kc, tt=tt, p=par: E.matmul(
                                PS(tt, 256), xlnT[:, kc * TL + tt * 128: kc * TL + (tt + 1) * 128],
                                vsl[:, p, kc * 256:(kc + 1) * 256],
                                start=(kc == 0), stop=(kc == KC - 1)), inc="sT",
                            fin=(kc == KC - 1))
                    em.wait("scalar", "sT")
                    for tt in range(NT):
                        vp = tt % 2
                        if tt >= 2:
                            em.wait("scalar", "sW")
                        em.op("scalar", lambda E, tt=tt, vp=vp: E.activation(
                            out=vstg[:, vp, :], in_=PS(tt, 256), func=AF.Copy), inc="sA")
                        em.op("scalar", lambda E, tt=tt, s=s_, vp=vp: E.dma_start(
                            out=v_loc[tt, :, s * 256:(s + 1) * 256], in_=vstg[:, vp, :]),
                            inc="sW", n=16)
                    slab_T[s_] = em.cnt["sT"]

        em.wait("gpsimd", "sS")
        em.wait("gpsimd", "sW")
        em.op("gpsimd", lambda E: E.collective_compute(
            "AllGather", ALU.bypass, ins=[qt_loc], outs=[qt_all], replica_groups=RG), inc="sCq")
        em.op("gpsimd", lambda E: E.collective_compute(
            "AllGather", ALU.bypass, ins=[kt_loc], outs=[kt_all], replica_groups=RG), inc="sCk")
        em.op("gpsimd", lambda E: E.collective_compute(
            "AllGather", ALU.bypass, ins=[v_loc], outs=[v_all], replica_groups=RG), inc="sC")
        coll1 = em.cnt["sC"]
        em.wait("gpsimd", "sCq")
        def rq(E):
            pid = PID["g"]
            return E.dma_start(
                out=qt_my.rearrange("c p (r t) -> c p r t", t=TL),
                in_=qt_all[:, bass.ts(pid, 4), :, :].rearrange("r c p t -> c p r t"))
        em.op("gpsimd", rq, inc="sG", n=16)
        em.wait("gpsimd", "sCk")
        def rk(E):
            pid = PID["g"]
            return E.dma_start(
                out=kt_my.rearrange("c p (r t) -> c p r t", t=TL),
                in_=kt_all[:, bass.ts(pid, 4), :, :].rearrange("r c p t -> c p r t"))
        em.op("gpsimd", rk, inc="sG", n=16)
        em.wait("gpsimd", "sC", coll1)
        def rv(E):
            pid = PID["g"]
            return E.dma_start(out=v_my[:, :, :, :], in_=v_all[:, :, :, bass.ts(pid, 512)])
        em.op("gpsimd", rv, inc="sG", n=16)
        repack_done = em.cnt["sG"]
        pre_repack = repack_done - 48
        rq_mark = repack_done - 32
        rk_mark = repack_done - 16
        rv_mark = repack_done

        with nc.sbuf_tensor("aosb", [128, 4 * S], BF16) as aosb, \
             nc.sbuf_tensor("masksb", [128, 4 * 512], BF16) as masks, \
             nc.sbuf_tensor("qbuf", [128, 2, 2 * 512], BF16) as qbuf, \
             nc.sbuf_tensor("ktbuf", [128, 2, 2 * 512], BF16) as ktbuf, \
             nc.sbuf_tensor("vbuf", [128, 2, 4 * 257], BF16) as vbuf, \
             nc.sbuf_tensor("ptbuf", [128, 4, 512], BF16) as ptbuf, \
             nc.sbuf_tensor("otmp", [128, 4 * 256], BF16) as otmp, \
             nc.sbuf_tensor("rcp", [128, 4], F32) as rcp:
            em.op("sync", lambda E: E.dma_start(out=masks[:, :], in_=masks_in), inc="sS", n=16)
            em.wait("vector", "sV")
            em.wait("vector", "sG", pre_repack)
            em.wait("vector", "sT")
            em.wait("vector", "sA")
            em.wait("vector", "sS")
            em.wait("vector", "sW")
            for p in range(2):
                for nt in range(4):
                    em.op("vector", lambda E, p=p, nt=nt: E.memset(
                        vbuf[:, p, nt * 257 + 256: nt * 257 + 257], 1.0), inc="sV")
            vones_V = em.cnt["sV"]
            em.wait("sync", "sC", coll1)
            em.wait("tensor", "sC", coll1)
            em.wait("sync", "sCq")
            em.wait("sync", "sCk")
            em.wait("tensor", "sCq")
            em.wait("tensor", "sCk")
            em.wait("tensor", "sV", vones_V)
            em.wait("tensor", "sS")

            for hh in range(2):
                for qc in range(16):
                    qpar = qc % 2
                    em.wait("sync", "sT")
                    em.wait("sync", "sG", rq_mark)
                    for c in range(2):
                        em.op("sync", lambda E, c=c, hh=hh, qc=qc, qp=qpar: E.dma_start(
                            out=qbuf[:, qp, c * 512:(c + 1) * 512],
                            in_=qt_my[2 * hh + c, :, qc * 512:(qc + 1) * 512]), inc="sS", n=16)
                    E_g = qc + 1
                    kv_kt = {}
                    kv_marks = {}
                    exp_marks = {}
                    av_marks = {}
                    logit_marks = {}
                    SB = (0, 1, 6, 7)
                    ntiles = 4 * E_g

                    def emit_kv(g):
                        gpar = g % 2
                        if g >= 2:
                            em.wait("sync", "sT", kv_marks[g - 2])
                        em.wait("sync", "sG", rk_mark)
                        for c in range(2):
                            em.op("sync", lambda E, c=c, hh=hh, g=g, gp=gpar: E.dma_start(
                                out=ktbuf[:, gp, c * 512:(c + 1) * 512],
                                in_=kt_my[2 * hh + c, :, g * 512:(g + 1) * 512]), inc="sS", n=16)
                        kv_kt[g] = em.cnt["sS"]
                        em.wait("sync", "sG", rv_mark)

                        def vd(E, hh=hh, g=g, gp=gpar):
                            src = v_my[g // 2, (g % 2) * 4:(g % 2) * 4 + 4, :,
                                       hh * 256:(hh + 1) * 256]
                            dst = vbuf.rearrange("p b (nt e) -> p b nt e", e=257)[:, gp, :, 0:256]
                            return E.dma_start(out=dst, in_=src.rearrange("nt p e -> p nt e"))
                        em.op("sync", vd, inc="sS", n=16)
                        return em.cnt["sS"]

                    kv_loads = {0: emit_kv(0)}

                    def emit_logits(idx, last):
                        g, nt = idx // 4, idx % 4
                        sp = SB[idx % 4]
                        gpar = g % 2
                        em.wait("tensor", "sS", kv_kt[g])
                        if idx >= 4:
                            em.wait("tensor", "sA", exp_marks[idx - 4])
                        for c in range(2):
                            em.op("tensor",
                                  lambda E, c=c, sp=sp, gp=gpar, nt=nt, qp=qpar: E.matmul(
                                      PS(sp),
                                      ktbuf[:, gp, c * 512 + nt * 128: c * 512 + nt * 128 + 128],
                                      qbuf[:, qp, c * 512:(c + 1) * 512],
                                      start=(c == 0), stop=(c == 1)), inc="sT",
                                  fin=(c == 1))
                        logit_marks[idx] = em.cnt["sT"]

                    emit_logits(0, ntiles == 1)
                    for idx in range(ntiles):
                        g, nt = idx // 4, idx % 4
                        sp = SB[idx % 4]
                        diag = (g == E_g - 1)
                        if nt == 0 and g + 1 < E_g and (g + 1) not in kv_loads:
                            kv_loads[g + 1] = emit_kv(g + 1)
                        if idx + 1 < ntiles:
                            emit_logits(idx + 1, idx + 1 == ntiles - 1)
                        em.wait("scalar", "sT", logit_marks[idx])
                        if idx >= 3:
                            k2 = idx - 3 if (idx - 3) % 2 == 1 else idx - 4
                            if k2 >= 0:
                                em.wait("scalar", "sT", av_marks[k2])
                        em.op("scalar", lambda E, sp=idx % 4, b=sp: E.activation(
                            out=ptbuf[:, sp, :], in_=PS(b), func=AF.Exp, scale=0.0625), inc="sA")
                        exp_marks[idx] = em.cnt["sA"]
                        if diag:
                            em.wait("vector", "sA", exp_marks[idx])
                            em.op("vector", lambda E, sp=idx % 4, nt=nt: E.tensor_mul(
                                ptbuf[:, sp, :], ptbuf[:, sp, :],
                                masks[:, nt * 512:(nt + 1) * 512]), inc="sV")
                            mask_mark = em.cnt["sV"]
                        em.wait("tensor", "sA", exp_marks[idx])
                        if diag:
                            em.wait("tensor", "sV", mask_mark)
                        em.wait("tensor", "sS", kv_loads[g])
                        for j in range(4):
                            em.op("tensor",
                                  lambda E, j=j, sp=idx % 4, gp=g % 2, g=g, nt=nt, qc=qc: E.matmul(
                                      PS(2 + j, 257), ptbuf[:, sp, j * 128:(j + 1) * 128],
                                      vbuf[:, gp, nt * 257: nt * 257 + 257],
                                      start=(g == 0 and nt == 0),
                                      stop=(g == qc and nt == 3)), inc="sT",
                                  fin=(j == 3 and (idx % 2 == 1 or idx == ntiles - 1)))
                        av_marks[idx] = em.cnt["sT"]
                        if nt == 3:
                            kv_marks[g] = em.cnt["sT"]
                    av_done = em.cnt["sT"]
                    em.wait("vector", "sT", av_done)
                    for j in range(4):
                        em.op("vector", lambda E, j=j: E.reciprocal(
                            rcp[:, j:j + 1], PS(2 + j, 1, 256)), inc="sV")
                    rec_mark = em.cnt["sV"]
                    em.wait("scalar", "sV", rec_mark)
                    for j in range(4):
                        em.op("scalar", lambda E, j=j: E.activation(
                            out=otmp[:, j * 256:(j + 1) * 256], in_=PS(2 + j, 256),
                            func=AF.Copy, scale=rcp[:, j:j + 1]), inc="sA")
                    norm_mark = em.cnt["sA"]
                    em.wait("tensor", "sA", norm_mark)
                    for j in range(4):
                        if j >= 1:
                            em.wait("tensor", "sV")
                        for c in range(2):
                            em.op("tensor", lambda E, j=j, c=c: E.transpose(
                                PSB(6 + c, 128),
                                otmp[:, j * 256 + c * 128: j * 256 + (c + 1) * 128],
                                ident[:, :]), inc="sT", fin=(c == 1))
                        tr_mark = em.cnt["sT"]
                        em.wait("vector", "sT", tr_mark)
                        for c in range(2):
                            em.op("vector", lambda E, j=j, c=c, hh=hh, qc=qc: E.tensor_copy(
                                aosb[:, (2 * hh + c) * S + qc * 512 + j * 128:
                                     (2 * hh + c) * S + qc * 512 + (j + 1) * 128],
                                PSB(6 + c, 128)), inc="sV")
                    em.wait("tensor", "sV")
                    em.op("tensor", lambda E: E.matmul(
                        PS(0, 128), ident[0:1, 0:128], ident[0:1, 0:128],
                        start=True, stop=True), inc="sT", fin=True)
                em.wait("sync", "sV")
                for c in range(2):
                    cc = 2 * hh + c
                    for r in range(NCORE):
                        em.op("sync", lambda E, cc=cc, r=r: E.dma_start(
                            out=aot_loc[cc, :, r * TL:(r + 1) * TL],
                            in_=aosb[:, cc * S + r * TL: cc * S + (r + 1) * TL]), inc="sS", n=16)
            aot_stores = em.cnt["sS"]

        em.wait("gpsimd", "sS", aot_stores)
        em.op("gpsimd", lambda E: E.collective_compute(
            "AllGather", ALU.bypass, ins=[aot_loc], outs=[aot_all], replica_groups=RG), inc="sC")
        coll2 = em.cnt["sC"]
        em.wait("gpsimd", "sC", coll2)
        def ra(E):
            pid = PID["g"]
            return E.dma_start(out=aot_my[:, :, :, :],
                               in_=aot_all[:, :, :, bass.ts(pid, TL)])
        em.op("gpsimd", ra, inc="sG", n=16)
        aot_rmark = {r: em.cnt["sG"] for r in range(NCORE)}

        with nc.sbuf_tensor("x1b", [128, 4 * D], F32) as x1, \
             nc.sbuf_tensor("xfT", [128, KC * 512], BF16) as xfT:
            em.wait("sync", "sC", coll2)
            for mc in range(2):
                if mc == 1:
                    em.wait("sync", "sS")
                for j in range(4):
                    tt = mc * 4 + j
                    em.op("sync", lambda E, tt=tt, j=j: E.dma_start(
                        out=x1.rearrange("p (j d) -> p j d", d=D)[:, j, :],
                        in_=x_in[tt * 128:(tt + 1) * 128, :]), inc="sS", n=16)
                with nc.sbuf_tensor(f"aosl{mc}", [128, KC * 512], BF16) as aosl, \
                     nc.sbuf_tensor(f"wosl{mc}", [128, 2, KC * 256], BF16) as wosl:
                    em.wait("sync", "sG", aot_rmark[NCORE - 1])
                    for dc in range(KC):
                        em.op("sync", lambda E, dc=dc, mc=mc: E.dma_start(
                            out=aosl[:, dc * 512:(dc + 1) * 512],
                            in_=aot_my[dc // 4, dc % 4, :, mc * 512:(mc + 1) * 512]), inc="sS", n=16)
                    x_ao_loaded = em.cnt["sS"]
                    wo_marks = {}
                    for ec in range(16):
                        par = ec % 2
                        if ec >= 2:
                            em.wait("scalar", "sT", wo_marks[ec - 2])
                        elif mc == 1:
                            em.wait("scalar", "sT")
                        for q in range(4):
                            src = wo16[:, ec * 256:(ec + 1) * 256].rearrange(
                                "(kc p) e -> p kc e", p=128)[:, q * 8:(q + 1) * 8, :]
                            dst = wosl.rearrange("p b (kc e) -> p b kc e", e=256)[
                                :, par, q * 8:(q + 1) * 8, :]
                            em.op("scalar", lambda E, s=src, d=dst: E.dma_start(out=d, in_=s),
                                  inc="sW", n=16)
                        em.wait("tensor", "sW")
                        em.wait("tensor", "sS", x_ao_loaded)
                        pb = (ec % 2) * 4
                        em.wait("tensor", "sV")
                        for kc in range(KC):
                            for j in range(4):
                                em.op("tensor", lambda E, kc=kc, j=j, p=par, pb=pb: E.matmul(
                                    PS(pb + j, 256),
                                    aosl[:, kc * 512 + j * 128: kc * 512 + (j + 1) * 128],
                                    wosl[:, p, kc * 256:(kc + 1) * 256],
                                    start=(kc == 0), stop=(kc == KC - 1)), inc="sT",
                                fin=(kc == KC - 1 and j == 3))
                        wo_marks[ec] = em.cnt["sT"]
                        em.wait("vector", "sT", wo_marks[ec])
                        for j in range(4):
                            em.op("vector", lambda E, j=j, ec=ec, pb=pb: E.tensor_add(
                                x1[:, j * D + ec * 256: j * D + (ec + 1) * 256],
                                PS(pb + j, 256),
                                x1[:, j * D + ec * 256: j * D + (ec + 1) * 256]), inc="sV")
                    x1_done_V = em.cnt["sV"]

                with nc.sbuf_tensor(f"xln2{mc}", [128, D], BF16) as xln2:
                    em.wait("vector", "sV", x1_done_V)
                    for j in range(4):
                        for c in range(8):
                            em.op("vector", lambda E, j=j, c=c: E.bn_stats(
                                stat[:, c, :], x1[:, j * D + 512 * c: j * D + 512 * (c + 1)]), inc="sV")
                        em.op("vector", lambda E: E.bn_aggr(mvv[:, :], stat[:, :, :]), inc="sV")
                        em.wait("scalar", "sV")
                        em.op("scalar", lambda E: E.activation(out=sdv[:, :], in_=mvv[:, 1:2], func=AF.Sqrt,
                                                               bias=epsv[:, :]), inc="sA")
                        em.wait("vector", "sA")
                        em.op("vector", lambda E: E.reciprocal(rsv[:, :], sdv[:, :]), inc="sV")
                        em.op("vector", lambda E, j=j: E.tensor_scalar(
                            xln2[:, :], x1[:, j * D:(j + 1) * D], mvv[:, 0:1], rsv[:, :],
                            ALU.subtract, ALU.mult), inc="sV")
                        em.wait("tensor", "sV")
                        for kg in range(8):
                            quad = (kg % 2) * 4
                            em.wait("tensor", "sV")
                            for jj in range(4):
                                kc = kg * 4 + jj
                                em.op("tensor", lambda E, kc=kc, b=quad + jj: E.transpose(
                                    PSB(b, 128), xln2[:, kc * 128:(kc + 1) * 128],
                                    ident[:, :]), inc="sT", fin=(jj == 3))
                            em.wait("vector", "sT")
                            for jj in range(4):
                                kc = kg * 4 + jj
                                dst = xfT.rearrange("p (kc t) -> p kc t", t=512)[
                                    :, kc, j * 128:(j + 1) * 128]
                                em.op("vector", lambda E, b=quad + jj, d=dst: E.tensor_copy(
                                    d, PSB(b, 128)), inc="sV")
                xf_done_V = em.cnt["sV"]

                em.wait("tensor", "sV", xf_done_V)
                for j in range(4):
                    for kc in range(KC):
                        em.op("tensor", lambda E, j=j, kc=kc: E.matmul(
                            PS(4 + j, NE),
                            xfT[:, kc * 512 + j * 128: kc * 512 + (j + 1) * 128],
                            wrt[:, kc * NE:(kc + 1) * NE],
                            start=(kc == 0), stop=(kc == KC - 1)), inc="sT",
                        fin=(kc == KC - 1))
                rl_mark = em.cnt["sT"]
                em.wait("vector", "sT", rl_mark)
                for j in range(4):
                    em.op("vector", lambda E, j=j: E.tensor_copy(r8a[:, :], PS(4 + j, NE)), inc="sV")
                    em.op("vector", lambda E: E.reduce_max(m1v[:, :], r8a[:, :], axis=AX), inc="sV")
                    em.op("vector", lambda E: E.tensor_single_scalar(
                        r8b[:, :], r8a[:, :], m1v[:, :], ALU.is_equal), inc="sV")
                    em.op("vector", lambda E: E.scalar_tensor_tensor(
                        r8c[:, :], r8b[:, :], -1e9, r8a[:, :], ALU.mult, ALU.add), inc="sV")
                    em.op("vector", lambda E: E.reduce_max(m2v[:, :], r8c[:, :], axis=AX), inc="sV")
                    em.op("vector", lambda E: E.tensor_sub(sdv[:, :], m2v[:, :], m1v[:, :]), inc="sV")
                    em.wait("scalar", "sV")
                    em.op("scalar", lambda E: E.activation(out=g2v[:, :], in_=sdv[:, :], func=AF.Sigmoid), inc="sA")
                    em.wait("vector", "sA")
                    em.op("vector", lambda E: E.tensor_scalar(
                        g1v[:, :], g2v[:, :], -1.0, 1.0, ALU.mult, ALU.add), inc="sV")
                    em.op("vector", lambda E: E.tensor_single_scalar(
                        r8a[:, :], r8c[:, :], m2v[:, :], ALU.is_equal), inc="sV")
                    em.op("vector", lambda E: E.tensor_scalar_mul(r8b[:, :], r8b[:, :], g1v[:, :]), inc="sV")
                    em.op("vector", lambda E: E.tensor_scalar_mul(r8a[:, :], r8a[:, :], g2v[:, :]), inc="sV")
                    em.op("vector", lambda E: E.tensor_add(r8b[:, :], r8b[:, :], r8a[:, :]), inc="sV")
                    em.op("vector", lambda E: E.tensor_copy(w8b[:, :], r8b[:, :]), inc="sV")
                    em.wait("tensor", "sV")
                    em.op("tensor", lambda E, j=j: E.transpose(
                        PSB(4 + j, 128)[0:NE, :], w8b[:, :], ident[:, :]), inc="sT", fin=True)
                wt_mark = em.cnt["sT"]
                with nc.sbuf_tensor(f"w8t{mc}", [NE, 512], BF16) as w8t:
                    em.wait("vector", "sT", wt_mark)
                    for j in range(4):
                        em.op("vector", lambda E, j=j: E.tensor_copy(
                            w8t[:, j * 128:(j + 1) * 128], PSB(4 + j, 128)[0:NE, :]), inc="sV")
                    em.wait("tensor", "sV")
                    em.op("tensor", lambda E: E.matmul(
                        PS(5), indm[:, :], w8t[:, :], start=True, stop=True), inc="sT", fin=True)
                    for kc in range(KC):
                        em.op("tensor", lambda E, kc=kc: E.matmul(
                            PS(6), at16[:, kc * 128:(kc + 1) * 128],
                            xfT[:, kc * 512:(kc + 1) * 512],
                            start=(kc == 0), stop=(kc == KC - 1)), inc="sT",
                        fin=(kc == KC - 1))
                    h_mark = em.cnt["sT"]
                    em.wait("vector", "sT", h_mark)
                    em.op("vector", lambda E: E.tensor_copy(wrepb[:, :], PS(5)), inc="sV")
                    em.op("vector", lambda E: E.tensor_mul(gt[:, :], PS(6), wrepb[:, :]), inc="sV")
                gt_mark = em.cnt["sV"]

                with nc.sbuf_tensor(f"wbsl{mc}", [128, 2, 16 * 512], BF16) as wbsl:
                    base_marks = {}
                    for ec in range(NE):
                        pb = (ec % 2) * 4
                        for half in range(2):
                            li = ec * 2 + half
                            par = li % 2
                            if li >= 2:
                                em.wait("scalar", "sT", base_marks[li - 2])
                            else:
                                em.wait("scalar", "sT")
                            km = half * 16
                            src = wbt16[km * 128:(km + 16) * 128, ec * 512:(ec + 1) * 512]
                            em.op("scalar", lambda E, s2=src, par=par: E.dma_start(
                                out=wbsl.rearrange("p b (kc e) -> p b kc e", e=512)[:, par, :, :],
                                in_=s2.rearrange("(kc p) e -> p kc e", p=128)), inc="sW", n=16)
                            em.wait("tensor", "sW")
                            if li == 0:
                                em.wait("tensor", "sV", gt_mark)
                                em.wait("tensor", "sG")
                            if half == 0:
                                em.wait("tensor", "sV")
                            for kc in range(16):
                                for j in range(4):
                                    em.op("tensor",
                                          lambda E, kc=kc, j=j, par=par, pb=pb, half=half: E.matmul(
                                              PS(pb + j),
                                              xfT[:, (half * 16 + kc) * 512 + j * 128:
                                                  (half * 16 + kc) * 512 + (j + 1) * 128],
                                              wbsl[:, par, kc * 512:(kc + 1) * 512],
                                              start=(half == 0 and kc == 0), stop=False), inc="sT")
                            base_marks[li] = em.cnt["sT"]
                        for j in range(4):
                            em.op("tensor", lambda E, j=j, ec=ec, pb=pb: E.matmul(
                                PS(pb + j), onesr[:, 0:128],
                                bb16[:, ec * 512:(ec + 1) * 512], start=False, stop=False), inc="sT")
                            em.op("tensor", lambda E, j=j, ec=ec, pb=pb: E.matmul(
                                PS(pb + j), gt[:, j * 128:(j + 1) * 128],
                                b3[:, ec * 512:(ec + 1) * 512], start=False, stop=True), inc="sT",
                            fin=(j == 3))
                        ec_mark = em.cnt["sT"]
                        em.wait("vector", "sT", ec_mark)
                        for j in range(4):
                            em.op("vector", lambda E, j=j, ec=ec, pb=pb: E.tensor_add(
                                x1[:, j * D + ec * 512: j * D + (ec + 1) * 512],
                                PS(pb + j),
                                x1[:, j * D + ec * 512: j * D + (ec + 1) * 512]), inc="sV")
                em.wait("sync", "sV")
                for j in range(4):
                    tt = mc * 4 + j
                    em.op("sync", lambda E, tt=tt, j=j: E.dma_start(
                        out=y_out[tt * 128:(tt + 1) * 128, :],
                        in_=x1.rearrange("p (j d) -> p j d", d=D)[:, j, :]), inc="sS", n=16)

        final_S = em.cnt["sS"]

        with nc.Block() as blk:
            @blk.sync
            def _(E):
                for f in em.ops["sync"]:
                    f(E)
                E.wait_ge(em.H["sS"], final_S)

            @blk.tensor
            def _(E):
                for f in em.ops["tensor"]:
                    f(E)

            @blk.vector
            def _(E):
                for f in em.ops["vector"]:
                    f(E)

            @blk.scalar
            def _(E):
                for f in em.ops["scalar"]:
                    f(E)

            @blk.gpsimd
            def _(E):
                for f in em.ops["gpsimd"]:
                    f(E)

    return nc


_CACHE = {}


def _consts():
    if "c" not in _CACHE:
        bf = ml_dtypes.bfloat16
        ident = np.eye(128, dtype=bf)
        masks = np.zeros((128, 4, 512), dtype=np.float32)
        n = np.arange(128)[:, None]
        m = np.arange(512)[None, :]
        for d in range(4):
            masks[:, d, :] = ((d * 128 + n) <= m).astype(np.float32)
        masks = masks.reshape(128, 4 * 512).astype(bf)
        indm = np.zeros((NE, 128), dtype=bf)
        for e in range(NE):
            indm[e, e * 16:(e + 1) * 16] = 1
        invf = 1.0 / 10000.0 ** (np.arange(0, ROT, 2, dtype=np.float64) / ROT)
        _CACHE["c"] = (ident, masks, indm, invf)
    return _CACHE["c"]


def kernel(L0_attn_hidden_states, L0_attn_Wq, L0_attn_Wk, L0_attn_Wv, L0_attn_Wo,
           L0_ffn_W_base, L0_ffn_b_base, L0_ffn_W_router, L0_ffn_A, L0_ffn_B):
    ident, masks, indm, invf = _consts()
    if "nc" not in _CACHE:
        _CACHE["nc"] = build_program()
    nc = _CACHE["nc"]

    f32 = np.float32
    x = np.ascontiguousarray(np.asarray(L0_attn_hidden_states, f32).reshape(S, D))
    wq = np.ascontiguousarray(np.asarray(L0_attn_Wq, f32))
    wk = np.ascontiguousarray(np.asarray(L0_attn_Wk, f32))
    wv = np.ascontiguousarray(np.asarray(L0_attn_Wv, f32))
    wo = np.ascontiguousarray(np.asarray(L0_attn_Wo, f32))
    wbt = np.ascontiguousarray(np.asarray(L0_ffn_W_base, f32).T)
    bb = np.ascontiguousarray(np.asarray(L0_ffn_b_base, f32).reshape(1, D))
    wrt_h = np.ascontiguousarray(np.asarray(L0_ffn_W_router, f32).T)
    amt = np.ascontiguousarray(np.asarray(L0_ffn_A, f32).reshape(128, D).T)
    bmr = np.ascontiguousarray(np.asarray(L0_ffn_B, f32).transpose(0, 2, 1).reshape(128, D))

    in_maps = []
    for c in range(NCORE):
        pos = np.arange(TL, dtype=np.float64) + c * TL
        ang = pos[None, :] * invf[:, None]
        cexp = np.repeat(np.cos(ang), 2, axis=0).astype(f32)
        sexp = np.repeat(np.sin(ang), 2, axis=0).astype(f32)
        in_maps.append({
            "x": x[c * TL:(c + 1) * TL],
            "wq": wq, "wk": wk, "wv": wv, "wo": wo, "wbt": wbt,
            "bb": bb, "wrt_h": wrt_h, "amt": amt, "bmr": bmr,
            "ident": ident, "masks": masks, "indm": indm,
            "cexp": cexp, "sexp": sexp,
        })
    res = run_bass_kernel_spmd(nc, in_maps, core_ids=list(range(NCORE)))
    y = np.concatenate([np.asarray(res.results[c]["y"]) for c in range(NCORE)], axis=0)
    return y.reshape(1, S, D).astype(f32)


# version 24
# speedup vs baseline: 1.0143x; 1.0143x over previous
import contextlib

import numpy as np
import ml_dtypes

import concourse.bass as bass
import concourse.mybir as mybir
from concourse.bass_utils import run_bass_kernel_spmd

F32 = mybir.dt.float32
BF16 = mybir.dt.bfloat16
AF = mybir.ActivationFunctionType
ALU = mybir.AluOpType
AX = mybir.AxisListType.X

NCORE = 8
D = 4096
S = 8192
TL = S // NCORE
NT = TL // 128
KC = D // 128
ECH = D // 128
HEADS = 16
HD = 256
ROT = 64
NE = 8
NR = 16
EPS = 1e-5

ENGS = ("sync", "tensor", "vector", "scalar", "gpsimd")


class Em:

    def __init__(self):
        self.ops = {e: [] for e in ENGS}
        self.cnt = {}
        self.waited = {e: {} for e in ENGS}
        self.H = {}

    def sems(self, *names):
        for n in names:
            self.cnt[n] = 0

    def op(self, eng, fn, inc=None, n=1, fin=False):
        if inc is not None:
            self.cnt[inc] += n
            if (eng in ("vector", "scalar") or fin) and n == 1:
                def g(E, f=fn, s=inc, k=n):
                    f(E)
                    E.maybe_drain_then_inc((self.H[s], k))
                self.ops[eng].append(g)
            else:
                self.ops[eng].append(
                    lambda E, f=fn, s=inc, k=n: f(E).then_inc(self.H[s], k))
            return self.cnt[inc]
        self.ops[eng].append(fn)
        return None

    def wait(self, eng, name, val=None):
        v = self.cnt[name] if val is None else val
        if v <= 0 or v <= self.waited[eng].get(name, 0):
            return
        self.waited[eng][name] = v
        self.ops[eng].append(lambda E, s=name, w=v: E.wait_ge(self.H[s], w))


def build_program():
    nc = bass.Bass("TRN2", target_bir_lowering=False, dynamic_dma_scratch_size=32768)

    x_in = nc.dram_tensor("x", [TL, D], F32, kind="ExternalInput").ap()
    wq_in = nc.dram_tensor("wq", [D, D], F32, kind="ExternalInput").ap()
    wk_in = nc.dram_tensor("wk", [D, D], F32, kind="ExternalInput").ap()
    wv_in = nc.dram_tensor("wv", [D, D], F32, kind="ExternalInput").ap()
    wo_in = nc.dram_tensor("wo", [D, D], F32, kind="ExternalInput").ap()
    wbt_in = nc.dram_tensor("wbt", [D, D], F32, kind="ExternalInput").ap()
    bb_in = nc.dram_tensor("bb", [1, D], F32, kind="ExternalInput").ap()
    wrt_in = nc.dram_tensor("wrt_h", [D, NE], F32, kind="ExternalInput").ap()
    amt_in = nc.dram_tensor("amt", [D, 128], F32, kind="ExternalInput").ap()
    bmr_in = nc.dram_tensor("bmr", [128, D], F32, kind="ExternalInput").ap()
    ident_in = nc.dram_tensor("ident", [128, 128], BF16, kind="ExternalInput").ap()
    masks_in = nc.dram_tensor("masks", [128, 4 * 512], BF16, kind="ExternalInput").ap()
    indm_in = nc.dram_tensor("indm", [NE, 128], BF16, kind="ExternalInput").ap()
    cexp_in = nc.dram_tensor("cexp", [ROT, TL], F32, kind="ExternalInput").ap()
    sexp_in = nc.dram_tensor("sexp", [ROT, TL], F32, kind="ExternalInput").ap()
    y_out = nc.dram_tensor("y", [TL, D], F32, kind="ExternalOutput").ap()

    qt_loc = nc.dram_tensor("qt_loc", [ECH, 128, TL], BF16, kind="Internal").ap()
    kt_loc = nc.dram_tensor("kt_loc", [ECH, 128, TL], BF16, kind="Internal").ap()
    v_loc = nc.dram_tensor("v_loc", [NT, 128, D], BF16, kind="Internal").ap()
    qt_all = nc.dram_tensor("qt_all", [NCORE, ECH, 128, TL], BF16, kind="Internal",
                            addr_space="Shared").ap()
    kt_all = nc.dram_tensor("kt_all", [NCORE, ECH, 128, TL], BF16, kind="Internal",
                            addr_space="Shared").ap()
    v_all = nc.dram_tensor("v_all", [NCORE, NT, 128, D], BF16, kind="Internal",
                           addr_space="Shared").ap()
    aot_loc = nc.dram_tensor("aot_loc", [4, 128, S], BF16, kind="Internal").ap()
    aot_all = nc.dram_tensor("aot_all", [NCORE, 4, 128, S], BF16, kind="Internal",
                             addr_space="Shared").ap()
    kt_my = nc.dram_tensor("kt_my", [4, 128, S], BF16, kind="Internal").ap()
    qt_my = nc.dram_tensor("qt_my", [4, 128, S], BF16, kind="Internal").ap()
    v_my = nc.dram_tensor("v_my", [NCORE, NT, 128, 512], BF16, kind="Internal").ap()
    aot_my = nc.dram_tensor("aot_my", [NCORE, 4, 128, TL], BF16, kind="Internal").ap()
    wbt16 = nc.dram_tensor("wbt16", [D, D], BF16, kind="Internal").ap()
    wo16 = nc.dram_tensor("wo16", [D, D], BF16, kind="Internal").ap()
    RG = [list(range(NCORE))]

    em = Em()
    em.sems("sS", "sG", "sW", "sT", "sV", "sA", "sC", "sCq", "sCk")
    PID = {}

    es = contextlib.ExitStack()
    with es:
        sb = lambda name, shape, dt: es.enter_context(nc.sbuf_tensor(name, shape, dt))
        ident = sb("identt", [128, 128], BF16)
        indm = sb("indmt", [NE, 128], BF16)
        at16 = sb("at16", [128, KC * 128], BF16)
        b3 = sb("b3", [128, KC * 128], BF16)
        wrt = sb("wrt", [128, KC * NE], BF16)
        bb16 = sb("bb16", [1, D], BF16)
        onesr = sb("onesr", [1, 128], BF16)
        stat = sb("stat", [128, 8, 6], F32)
        mvv = sb("mvv", [128, 2], F32)
        sdv = sb("sdv", [128, 1], F32)
        rsv = sb("rsv", [128, 1], F32)
        r8a = sb("r8a", [128, 8], F32)
        r8b = sb("r8b", [128, 8], F32)
        r8c = sb("r8c", [128, 8], F32)
        m1v = sb("m1v", [128, 1], F32)
        m2v = sb("m2v", [128, 1], F32)
        g2v = sb("g2v", [128, 1], F32)
        g1v = sb("g1v", [128, 1], F32)
        w8b = sb("w8b", [128, 8], BF16)
        gt = sb("gtb", [128, 512], BF16)
        wrepb = sb("wrepb", [128, 512], F32)
        epsv = sb("epsv", [128, 1], F32)

        psum = es.enter_context(nc.psum_tensor("ps", [128, 4096], F32))
        PS = lambda bank, w=512, off=0: psum[:, bank * 512 + off: bank * 512 + off + w]
        PSB = lambda bank, wb=128, offb=0: psum[:, bank * 512 + offb // 2:
                                                bank * 512 + offb // 2 + wb // 2].bitcast(BF16)

        for s in ("sS", "sG", "sW", "sT", "sV", "sA", "sC", "sCq", "sCk"):
            em.H[s] = es.enter_context(nc.semaphore(s))

        em.op("sync", lambda E: E.dma_start(out=ident[:, :], in_=ident_in), inc="sS", n=16)
        em.op("sync", lambda E: E.dma_start(out=indm[:, :], in_=indm_in), inc="sS", n=16)

        em.op("gpsimd", lambda E: E.dma_start(out=bb16[:, :], in_=bb_in), inc="sG", n=16)
        for i in range(4):
            em.op("gpsimd", lambda E, i=i: E.dma_start(
                out=wbt16[i * 1024:(i + 1) * 1024, :], in_=wbt_in[i * 1024:(i + 1) * 1024, :]),
                inc="sG", n=16)
            em.op("gpsimd", lambda E, i=i: E.dma_start(
                out=wo16[i * 1024:(i + 1) * 1024, :], in_=wo_in[i * 1024:(i + 1) * 1024, :]),
                inc="sG", n=16)

        em.op("vector", lambda E: E.memset(onesr[:, :], 1.0), inc="sV")
        em.op("vector", lambda E: E.memset(epsv[:, :], EPS), inc="sV")

        with nc.sbuf_tensor("xlnT", [128, KC * TL], BF16) as xlnT, \
             nc.sbuf_tensor("cexpt", [ROT, TL], F32) as cexp, \
             nc.sbuf_tensor("sexpt", [ROT, TL], F32) as sexp:
            em.op("sync", lambda E: E.dma_start(out=cexp[:, :], in_=cexp_in), inc="sS", n=16)
            em.op("sync", lambda E: E.dma_start(out=sexp[:, :], in_=sexp_in), inc="sS", n=16)
            with nc.sbuf_tensor("xbuf", [128, 2, D], F32) as xbuf, \
                 nc.sbuf_tensor("xln", [128, D], BF16) as xln:
                with nc.sbuf_tensor("prep32", [128, KC * 128], F32) as prep32:
                    em.op("sync", lambda E: E.dma_start(out=prep32[:, :], in_=bmr_in), inc="sS", n=16)
                    em.wait("vector", "sS")
                    em.op("vector", lambda E: E.tensor_scalar(
                        b3[:, :], prep32[:, :], 2.0, None, ALU.mult), inc="sV")
                    b3_done = em.cnt["sV"]
                    em.wait("sync", "sV", b3_done)
                    em.op("sync", lambda E: E.dma_start(
                        out=prep32.rearrange("p (kc j) -> p kc j", j=128),
                        in_=amt_in.rearrange("(kc p) j -> p kc j", p=128)), inc="sS", n=16)
                    em.wait("vector", "sS")
                    em.op("vector", lambda E: E.tensor_copy(at16[:, :], prep32[:, :]), inc="sV")
                    at_done = em.cnt["sV"]
                    em.wait("sync", "sV", at_done)
                    em.op("sync", lambda E: E.dma_start(
                        out=prep32.rearrange("p (kc e) -> p kc e", e=NE)[:, 0:KC, :],
                        in_=wrt_in.rearrange("(kc p) e -> p kc e", p=128)), inc="sS", n=16)
                    em.wait("vector", "sS")
                    em.op("vector", lambda E: E.tensor_copy(wrt[:, :], prep32[:, 0:KC * NE]), inc="sV")


                for tt in range(NT):
                    par = tt % 2
                    if tt >= 2:
                        em.wait("sync", "sA")
                    em.op("sync", lambda E, tt=tt, p=par: E.dma_start(
                        out=xbuf[:, p, :], in_=x_in[tt * 128:(tt + 1) * 128, :]), inc="sS", n=16)
                    em.wait("vector", "sS")
                    for c in range(8):
                        em.op("vector", lambda E, p=par, c=c: E.bn_stats(
                            stat[:, c, :], xbuf[:, p, 512 * c:512 * (c + 1)]), inc="sV")
                    em.op("vector", lambda E: E.bn_aggr(mvv[:, :], stat[:, :, :]), inc="sV")
                    em.wait("scalar", "sV")
                    em.op("scalar", lambda E: E.activation(out=sdv[:, :], in_=mvv[:, 1:2], func=AF.Sqrt,
                                                           bias=epsv[:, :]), inc="sA")
                    em.wait("vector", "sA")
                    em.op("vector", lambda E: E.reciprocal(rsv[:, :], sdv[:, :]), inc="sV")
                    em.op("vector", lambda E, p=par: E.tensor_scalar(
                        xln[:, :], xbuf[:, p, :], mvv[:, 0:1], rsv[:, :], ALU.subtract, ALU.mult), inc="sV")
                    em.wait("scalar", "sV")
                    em.op("scalar", lambda E: E.activation(out=sdv[:, :], in_=sdv[:, :], func=AF.Copy), inc="sA")
                    em.wait("tensor", "sV")
                    for kg in range(8):
                        quad = (kg % 2) * 4
                        em.wait("tensor", "sV")
                        for j in range(4):
                            kc = kg * 4 + j
                            em.op("tensor", lambda E, kc=kc, b=quad + j: E.transpose(
                                PSB(b, 128), xln[:, kc * 128:(kc + 1) * 128], ident[:, :]), inc="sT",
                                fin=(j == 3))
                        em.wait("vector", "sT")
                        for j in range(4):
                            kc = kg * 4 + j
                            dst = xlnT.rearrange("p (kc t) -> p kc t", t=TL)[
                                :, kc, tt * 128:(tt + 1) * 128]
                            em.op("vector", lambda E, b=quad + j, d=dst: E.tensor_copy(
                                d, PSB(b, 128)), inc="sV")

            def qk_phase(w_src, dst_loc, tag):
                with nc.sbuf_tensor("wsl" + tag, [128, 2, KC * 256], BF16) as wsl, \
                     nc.sbuf_tensor("wslf" + tag, [128, 2, KC * 128], F32) as wslf, \
                     nc.sbuf_tensor("stgf" + tag, [128, TL], F32) as stgf, \
                     nc.sbuf_tensor("stg16" + tag, [128, TL], BF16) as stg16, \
                     nc.sbuf_tensor("rot1" + tag, [ROT, 512], F32) as rot1, \
                     nc.sbuf_tensor("rot2" + tag, [ROT, 512], F32) as rot2, \
                     nc.sbuf_tensor("rotsw" + tag, [ROT, 512], F32) as rotsw:
                    slab_T = {}
                    cast_V = {}
                    em.wait("sync", "sT")
                    em.wait("sync", "sV")
                    em.wait("sync", "sA")
                    em.wait("sync", "sW")

                    def emit_half(s_, et):
                        hpar = (s_ * 2 + et) % 2
                        if (s_, et) >= (1, 0):
                            prev = (s_ * 2 + et) - 2
                            if prev >= 0:
                                em.wait("sync", "sV", cast_V.get(prev, 0))
                        src = w_src[:, s_ * 256 + et * 128: s_ * 256 + (et + 1) * 128].rearrange(
                            "(kc p) e -> p kc e", p=128)
                        em.op("sync", lambda E, src=src, hp=hpar: E.dma_start(
                            out=wslf.rearrange("p b (kc e) -> p b kc e", e=128)[:, hp, :, :],
                            in_=src), inc="sS", n=16)
                        ld = em.cnt["sS"]
                        em.wait("vector", "sS", ld)
                        if s_ >= 2:
                            em.wait("vector", "sT", slab_T[s_ - 2])
                        dst = wsl.rearrange("p b (kc e) -> p b kc e", e=256)[
                            :, s_ % 2, :, et * 128:(et + 1) * 128]
                        em.op("vector", lambda E, hp=hpar, d=dst: E.tensor_copy(
                            d, wslf.rearrange("p b (kc e) -> p b kc e", e=128)[:, hp, :, :]),
                            inc="sV")
                        cast_V[s_ * 2 + et] = em.cnt["sV"]

                    emit_half(0, 0)
                    emit_half(0, 1)
                    for s_ in range(16):
                        par = s_ % 2
                        if s_ + 1 < 16:
                            emit_half(s_ + 1, 0)
                            emit_half(s_ + 1, 1)
                        for et in range(2):
                            e_g = s_ * 2 + et
                            em.wait("tensor", "sV", cast_V[s_ * 2 + et])
                            for ch in range(2):
                                bank = et * 2 + ch
                                em.wait("tensor", "sA")
                                for kc in range(KC):
                                    em.op("tensor",
                                          lambda E, kc=kc, b=bank, p=par, et=et, ch=ch: E.matmul(
                                              PS(b),
                                              wsl[:, p, kc * 256 + et * 128: kc * 256 + (et + 1) * 128],
                                              xlnT[:, kc * TL + ch * 512: kc * TL + (ch + 1) * 512],
                                              start=(kc == 0), stop=(kc == KC - 1)), inc="sT",
                                          fin=(kc == KC - 1))
                            em.wait("scalar", "sT")
                            em.wait("scalar", "sV")
                            for ch in range(2):
                                bank = et * 2 + ch
                                em.op("scalar", lambda E, b=bank, ch=ch: E.activation(
                                    out=stgf[:, ch * 512:(ch + 1) * 512], in_=PS(b), func=AF.Copy),
                                    inc="sA")
                            rot = (e_g % 2 == 0)
                            if rot:
                                em.wait("vector", "sA")
                                for ch in range(2):
                                    sl = slice(ch * 512, (ch + 1) * 512)
                                    em.op("vector", lambda E, sl=sl: E.stream_shuffle(
                                        rotsw[:, :], stgf[0:ROT, sl], [p ^ 1 for p in range(32)]), inc="sV")
                                    em.op("vector", lambda E, sl=sl: E.tensor_mul(
                                        rot1[:, :], stgf[0:ROT, sl], cexp[:, sl]), inc="sV")
                                    em.op("vector", lambda E, sl=sl: E.tensor_mul(
                                        rot2[:, :], rotsw[:, :], sexp[:, sl]), inc="sV")
                                    em.op("vector", lambda E, sl=sl: E.tensor_add(
                                        stgf[0:ROT, sl], rot1[:, :], rot2[:, :]), inc="sV")
                            em.wait("vector", "sA")
                            em.wait("vector", "sW")
                            em.op("vector", lambda E: E.tensor_copy(stg16[:, :], stgf[:, :]), inc="sV")
                            em.wait("scalar", "sV")
                            em.op("scalar", lambda E, e_g=e_g, d=dst_loc: E.dma_start(
                                out=d[e_g, :, :], in_=stg16[:, :]), inc="sW", n=16)
                        slab_T[s_] = em.cnt["sT"]

            qk_phase(wq_in, qt_loc, "q")
            em.ops["gpsimd"].insert(0, lambda E: PID.__setitem__("g", E.partition_id()))
            qk_phase(wk_in, kt_loc, "k")

            with nc.sbuf_tensor("vsl", [128, 2, KC * 256], BF16) as vsl, \
                 nc.sbuf_tensor("vslf", [128, 2, KC * 128], F32) as vslf, \
                 nc.sbuf_tensor("vstg", [128, 2, 256], BF16) as vstg:
                slab_T = {}
                cast_V = {}
                em.wait("sync", "sT")
                em.wait("sync", "sV")
                em.wait("sync", "sA")
                em.wait("sync", "sW")

                def emit_vhalf(s_, et):
                    hpar = (s_ * 2 + et) % 2
                    prev = (s_ * 2 + et) - 2
                    if prev >= 0:
                        em.wait("sync", "sV", cast_V.get(prev, 0))
                    src = wv_in[:, s_ * 256 + et * 128: s_ * 256 + (et + 1) * 128].rearrange(
                        "(kc p) e -> p kc e", p=128)
                    em.op("sync", lambda E, src=src, hp=hpar: E.dma_start(
                        out=vslf.rearrange("p b (kc e) -> p b kc e", e=128)[:, hp, :, :],
                        in_=src), inc="sS", n=16)
                    ld = em.cnt["sS"]
                    em.wait("vector", "sS", ld)
                    if s_ >= 2:
                        em.wait("vector", "sT", slab_T[s_ - 2])
                    dst = vsl.rearrange("p b (kc e) -> p b kc e", e=256)[
                        :, s_ % 2, :, et * 128:(et + 1) * 128]
                    em.op("vector", lambda E, hp=hpar, d=dst: E.tensor_copy(
                        d, vslf.rearrange("p b (kc e) -> p b kc e", e=128)[:, hp, :, :]),
                        inc="sV")
                    cast_V[s_ * 2 + et] = em.cnt["sV"]

                emit_vhalf(0, 0)
                emit_vhalf(0, 1)
                for s_ in range(16):
                    par = s_ % 2
                    if s_ + 1 < 16:
                        emit_vhalf(s_ + 1, 0)
                        emit_vhalf(s_ + 1, 1)
                    em.wait("tensor", "sV", cast_V[s_ * 2 + 1])
                    em.wait("tensor", "sA")
                    for tt in range(NT):
                        for kc in range(KC):
                            em.op("tensor", lambda E, kc=kc, tt=tt, p=par: E.matmul(
                                PS(tt, 256), xlnT[:, kc * TL + tt * 128: kc * TL + (tt + 1) * 128],
                                vsl[:, p, kc * 256:(kc + 1) * 256],
                                start=(kc == 0), stop=(kc == KC - 1)), inc="sT",
                            fin=(kc == KC - 1))
                    em.wait("scalar", "sT")
                    for tt in range(NT):
                        vp = tt % 2
                        if tt >= 2:
                            em.wait("scalar", "sW")
                        em.op("scalar", lambda E, tt=tt, vp=vp: E.activation(
                            out=vstg[:, vp, :], in_=PS(tt, 256), func=AF.Copy), inc="sA")
                        em.op("scalar", lambda E, tt=tt, s=s_, vp=vp: E.dma_start(
                            out=v_loc[tt, :, s * 256:(s + 1) * 256], in_=vstg[:, vp, :]),
                            inc="sW", n=16)
                    slab_T[s_] = em.cnt["sT"]

        em.wait("gpsimd", "sS")
        em.wait("gpsimd", "sW")
        em.op("gpsimd", lambda E: E.collective_compute(
            "AllGather", ALU.bypass, ins=[qt_loc], outs=[qt_all], replica_groups=RG), inc="sCq")
        em.op("gpsimd", lambda E: E.collective_compute(
            "AllGather", ALU.bypass, ins=[kt_loc], outs=[kt_all], replica_groups=RG), inc="sCk")
        em.op("gpsimd", lambda E: E.collective_compute(
            "AllGather", ALU.bypass, ins=[v_loc], outs=[v_all], replica_groups=RG), inc="sC")
        coll1 = em.cnt["sC"]
        em.wait("gpsimd", "sCq")
        def rq(E):
            pid = PID["g"]
            return E.dma_start(
                out=qt_my.rearrange("c p (r t) -> c p r t", t=TL),
                in_=qt_all[:, bass.ts(pid, 4), :, :].rearrange("r c p t -> c p r t"))
        em.op("gpsimd", rq, inc="sG", n=16)
        em.wait("gpsimd", "sCk")
        def rk(E):
            pid = PID["g"]
            return E.dma_start(
                out=kt_my.rearrange("c p (r t) -> c p r t", t=TL),
                in_=kt_all[:, bass.ts(pid, 4), :, :].rearrange("r c p t -> c p r t"))
        em.op("gpsimd", rk, inc="sG", n=16)
        em.wait("gpsimd", "sC", coll1)
        def rv(E):
            pid = PID["g"]
            return E.dma_start(out=v_my[:, :, :, :], in_=v_all[:, :, :, bass.ts(pid, 512)])
        em.op("gpsimd", rv, inc="sG", n=16)
        repack_done = em.cnt["sG"]
        pre_repack = repack_done - 48
        rq_mark = repack_done - 32
        rk_mark = repack_done - 16
        rv_mark = repack_done

        with nc.sbuf_tensor("aosb", [128, 4 * S], BF16) as aosb, \
             nc.sbuf_tensor("masksb", [128, 4 * 512], BF16) as masks, \
             nc.sbuf_tensor("qbuf", [128, 2, 2 * 512], BF16) as qbuf, \
             nc.sbuf_tensor("ktbuf", [128, 2, 2 * 512], BF16) as ktbuf, \
             nc.sbuf_tensor("vbuf", [128, 2, 4 * 257], BF16) as vbuf, \
             nc.sbuf_tensor("ptbuf", [128, 4, 512], BF16) as ptbuf, \
             nc.sbuf_tensor("otmp", [128, 4 * 256], BF16) as otmp, \
             nc.sbuf_tensor("rcp", [128, 4], F32) as rcp:
            em.op("sync", lambda E: E.dma_start(out=masks[:, :], in_=masks_in), inc="sS", n=16)
            em.wait("vector", "sV")
            em.wait("vector", "sG", pre_repack)
            em.wait("vector", "sT")
            em.wait("vector", "sA")
            em.wait("vector", "sS")
            em.wait("vector", "sW")
            for p in range(2):
                for nt in range(4):
                    em.op("vector", lambda E, p=p, nt=nt: E.memset(
                        vbuf[:, p, nt * 257 + 256: nt * 257 + 257], 1.0), inc="sV")
            vones_V = em.cnt["sV"]
            em.wait("sync", "sC", coll1)
            em.wait("tensor", "sC", coll1)
            em.wait("sync", "sCq")
            em.wait("sync", "sCk")
            em.wait("tensor", "sCq")
            em.wait("tensor", "sCk")
            em.wait("tensor", "sV", vones_V)
            em.wait("tensor", "sS")

            for hh in range(2):
                for qc in range(16):
                    qpar = qc % 2
                    em.wait("sync", "sT")
                    em.wait("sync", "sG", rq_mark)
                    for c in range(2):
                        em.op("sync", lambda E, c=c, hh=hh, qc=qc, qp=qpar: E.dma_start(
                            out=qbuf[:, qp, c * 512:(c + 1) * 512],
                            in_=qt_my[2 * hh + c, :, qc * 512:(qc + 1) * 512]), inc="sS", n=16)
                    E_g = qc + 1
                    kv_kt = {}
                    kv_marks = {}
                    exp_marks = {}
                    av_marks = {}
                    logit_marks = {}
                    SB = (0, 1, 6, 7)
                    ntiles = 4 * E_g

                    def emit_kv(g):
                        gpar = g % 2
                        if g >= 2:
                            em.wait("sync", "sT", kv_marks[g - 2])
                        em.wait("sync", "sG", rk_mark)
                        for c in range(2):
                            em.op("sync", lambda E, c=c, hh=hh, g=g, gp=gpar: E.dma_start(
                                out=ktbuf[:, gp, c * 512:(c + 1) * 512],
                                in_=kt_my[2 * hh + c, :, g * 512:(g + 1) * 512]), inc="sS", n=16)
                        kv_kt[g] = em.cnt["sS"]
                        em.wait("sync", "sG", rv_mark)

                        def vd(E, hh=hh, g=g, gp=gpar):
                            src = v_my[g // 2, (g % 2) * 4:(g % 2) * 4 + 4, :,
                                       hh * 256:(hh + 1) * 256]
                            dst = vbuf.rearrange("p b (nt e) -> p b nt e", e=257)[:, gp, :, 0:256]
                            return E.dma_start(out=dst, in_=src.rearrange("nt p e -> p nt e"))
                        em.op("sync", vd, inc="sS", n=16)
                        return em.cnt["sS"]

                    kv_loads = {0: emit_kv(0)}
                    em.wait("tensor", "sV")

                    def emit_logits(idx, last):
                        g, nt = idx // 4, idx % 4
                        sp = SB[idx % 4]
                        gpar = g % 2
                        em.wait("tensor", "sS", kv_kt[g])
                        if idx >= 4:
                            em.wait("tensor", "sA", exp_marks[idx - 4])
                        for c in range(2):
                            em.op("tensor",
                                  lambda E, c=c, sp=sp, gp=gpar, nt=nt, qp=qpar: E.matmul(
                                      PS(sp),
                                      ktbuf[:, gp, c * 512 + nt * 128: c * 512 + nt * 128 + 128],
                                      qbuf[:, qp, c * 512:(c + 1) * 512],
                                      start=(c == 0), stop=(c == 1)), inc="sT",
                                  fin=(c == 1))
                        logit_marks[idx] = em.cnt["sT"]

                    emit_logits(0, ntiles == 1)
                    for idx in range(ntiles):
                        g, nt = idx // 4, idx % 4
                        sp = SB[idx % 4]
                        diag = (g == E_g - 1)
                        if nt == 0 and g + 1 < E_g and (g + 1) not in kv_loads:
                            kv_loads[g + 1] = emit_kv(g + 1)
                        if idx + 1 < ntiles:
                            emit_logits(idx + 1, idx + 1 == ntiles - 1)
                        em.wait("scalar", "sT", logit_marks[idx])
                        if idx >= 3:
                            k2 = idx - 3 if (idx - 3) % 2 == 1 else idx - 4
                            if k2 >= 0:
                                em.wait("scalar", "sT", av_marks[k2])
                        em.op("scalar", lambda E, sp=idx % 4, b=sp: E.activation(
                            out=ptbuf[:, sp, :], in_=PS(b), func=AF.Exp, scale=0.0625), inc="sA")
                        exp_marks[idx] = em.cnt["sA"]
                        if diag:
                            em.wait("vector", "sA", exp_marks[idx])
                            em.op("vector", lambda E, sp=idx % 4, nt=nt: E.tensor_mul(
                                ptbuf[:, sp, :], ptbuf[:, sp, :],
                                masks[:, nt * 512:(nt + 1) * 512]), inc="sV")
                            mask_mark = em.cnt["sV"]
                        em.wait("tensor", "sA", exp_marks[idx])
                        if diag:
                            em.wait("tensor", "sV", mask_mark)
                        em.wait("tensor", "sS", kv_loads[g])
                        for j in range(4):
                            em.op("tensor",
                                  lambda E, j=j, sp=idx % 4, gp=g % 2, g=g, nt=nt, qc=qc: E.matmul(
                                      PS(2 + j, 257), ptbuf[:, sp, j * 128:(j + 1) * 128],
                                      vbuf[:, gp, nt * 257: nt * 257 + 257],
                                      start=(g == 0 and nt == 0),
                                      stop=(g == qc and nt == 3)), inc="sT",
                                  fin=(j == 3 and (idx % 2 == 1 or idx == ntiles - 1)))
                        av_marks[idx] = em.cnt["sT"]
                        if nt == 3:
                            kv_marks[g] = em.cnt["sT"]
                    av_done = em.cnt["sT"]
                    em.wait("vector", "sT", av_done)
                    for j in range(4):
                        em.op("vector", lambda E, j=j: E.reciprocal(
                            rcp[:, j:j + 1], PS(2 + j, 1, 256)), inc="sV")
                    rec_mark = em.cnt["sV"]
                    em.wait("scalar", "sV", rec_mark)
                    for j in range(4):
                        em.op("scalar", lambda E, j=j: E.activation(
                            out=otmp[:, j * 256:(j + 1) * 256], in_=PS(2 + j, 256),
                            func=AF.Copy, scale=rcp[:, j:j + 1]), inc="sA")
                    norm_mark = em.cnt["sA"]
                    em.wait("tensor", "sA", norm_mark)
                    for j in range(4):
                        if j >= 1:
                            em.wait("tensor", "sV")
                        for c in range(2):
                            em.op("tensor", lambda E, j=j, c=c: E.transpose(
                                PSB(6 + c, 128),
                                otmp[:, j * 256 + c * 128: j * 256 + (c + 1) * 128],
                                ident[:, :]), inc="sT", fin=(c == 1))
                        tr_mark = em.cnt["sT"]
                        em.wait("vector", "sT", tr_mark)
                        for c in range(2):
                            em.op("vector", lambda E, j=j, c=c, hh=hh, qc=qc: E.tensor_copy(
                                aosb[:, (2 * hh + c) * S + qc * 512 + j * 128:
                                     (2 * hh + c) * S + qc * 512 + (j + 1) * 128],
                                PSB(6 + c, 128)), inc="sV")
                    em.wait("tensor", "sV")
                    em.op("tensor", lambda E: E.matmul(
                        PS(0, 128), ident[0:1, 0:128], ident[0:1, 0:128],
                        start=True, stop=True), inc="sT", fin=True)
                em.wait("sync", "sV")
                for c in range(2):
                    cc = 2 * hh + c
                    for r in range(NCORE):
                        em.op("sync", lambda E, cc=cc, r=r: E.dma_start(
                            out=aot_loc[cc, :, r * TL:(r + 1) * TL],
                            in_=aosb[:, cc * S + r * TL: cc * S + (r + 1) * TL]), inc="sS", n=16)
            aot_stores = em.cnt["sS"]

        em.wait("gpsimd", "sS", aot_stores)
        em.op("gpsimd", lambda E: E.collective_compute(
            "AllGather", ALU.bypass, ins=[aot_loc], outs=[aot_all], replica_groups=RG), inc="sC")
        coll2 = em.cnt["sC"]
        em.wait("gpsimd", "sC", coll2)
        def ra(E):
            pid = PID["g"]
            return E.dma_start(out=aot_my[:, :, :, :],
                               in_=aot_all[:, :, :, bass.ts(pid, TL)])
        em.op("gpsimd", ra, inc="sG", n=16)
        aot_rmark = {r: em.cnt["sG"] for r in range(NCORE)}

        with nc.sbuf_tensor("x1b", [128, 4 * D], F32) as x1, \
             nc.sbuf_tensor("xfT", [128, KC * 512], BF16) as xfT:
            em.wait("sync", "sC", coll2)
            for mc in range(2):
                if mc == 1:
                    em.wait("sync", "sS")
                for j in range(4):
                    tt = mc * 4 + j
                    em.op("sync", lambda E, tt=tt, j=j: E.dma_start(
                        out=x1.rearrange("p (j d) -> p j d", d=D)[:, j, :],
                        in_=x_in[tt * 128:(tt + 1) * 128, :]), inc="sS", n=16)
                with nc.sbuf_tensor(f"aosl{mc}", [128, KC * 512], BF16) as aosl, \
                     nc.sbuf_tensor(f"wosl{mc}", [128, 2, KC * 256], BF16) as wosl:
                    em.wait("sync", "sG", aot_rmark[NCORE - 1])
                    for dc in range(KC):
                        em.op("sync", lambda E, dc=dc, mc=mc: E.dma_start(
                            out=aosl[:, dc * 512:(dc + 1) * 512],
                            in_=aot_my[dc // 4, dc % 4, :, mc * 512:(mc + 1) * 512]), inc="sS", n=16)
                    x_ao_loaded = em.cnt["sS"]
                    wo_marks = {}
                    for ec in range(16):
                        par = ec % 2
                        if ec >= 2:
                            em.wait("scalar", "sT", wo_marks[ec - 2])
                        elif mc == 1:
                            em.wait("scalar", "sT")
                        for q in range(4):
                            src = wo16[:, ec * 256:(ec + 1) * 256].rearrange(
                                "(kc p) e -> p kc e", p=128)[:, q * 8:(q + 1) * 8, :]
                            dst = wosl.rearrange("p b (kc e) -> p b kc e", e=256)[
                                :, par, q * 8:(q + 1) * 8, :]
                            em.op("scalar", lambda E, s=src, d=dst: E.dma_start(out=d, in_=s),
                                  inc="sW", n=16)
                        em.wait("tensor", "sW")
                        em.wait("tensor", "sS", x_ao_loaded)
                        pb = (ec % 2) * 4
                        em.wait("tensor", "sV")
                        for kc in range(KC):
                            for j in range(4):
                                em.op("tensor", lambda E, kc=kc, j=j, p=par, pb=pb: E.matmul(
                                    PS(pb + j, 256),
                                    aosl[:, kc * 512 + j * 128: kc * 512 + (j + 1) * 128],
                                    wosl[:, p, kc * 256:(kc + 1) * 256],
                                    start=(kc == 0), stop=(kc == KC - 1)), inc="sT",
                                fin=(kc == KC - 1 and j == 3))
                        wo_marks[ec] = em.cnt["sT"]
                        em.wait("vector", "sT", wo_marks[ec])
                        for j in range(4):
                            em.op("vector", lambda E, j=j, ec=ec, pb=pb: E.tensor_add(
                                x1[:, j * D + ec * 256: j * D + (ec + 1) * 256],
                                PS(pb + j, 256),
                                x1[:, j * D + ec * 256: j * D + (ec + 1) * 256]), inc="sV")
                    x1_done_V = em.cnt["sV"]

                with nc.sbuf_tensor(f"xln2{mc}", [128, D], BF16) as xln2:
                    em.wait("vector", "sV", x1_done_V)
                    for j in range(4):
                        for c in range(8):
                            em.op("vector", lambda E, j=j, c=c: E.bn_stats(
                                stat[:, c, :], x1[:, j * D + 512 * c: j * D + 512 * (c + 1)]), inc="sV")
                        em.op("vector", lambda E: E.bn_aggr(mvv[:, :], stat[:, :, :]), inc="sV")
                        em.wait("scalar", "sV")
                        em.op("scalar", lambda E: E.activation(out=sdv[:, :], in_=mvv[:, 1:2], func=AF.Sqrt,
                                                               bias=epsv[:, :]), inc="sA")
                        em.wait("vector", "sA")
                        em.op("vector", lambda E: E.reciprocal(rsv[:, :], sdv[:, :]), inc="sV")
                        em.op("vector", lambda E, j=j: E.tensor_scalar(
                            xln2[:, :], x1[:, j * D:(j + 1) * D], mvv[:, 0:1], rsv[:, :],
                            ALU.subtract, ALU.mult), inc="sV")
                        em.wait("tensor", "sV")
                        for kg in range(8):
                            quad = (kg % 2) * 4
                            em.wait("tensor", "sV")
                            for jj in range(4):
                                kc = kg * 4 + jj
                                em.op("tensor", lambda E, kc=kc, b=quad + jj: E.transpose(
                                    PSB(b, 128), xln2[:, kc * 128:(kc + 1) * 128],
                                    ident[:, :]), inc="sT", fin=(jj == 3))
                            em.wait("vector", "sT")
                            for jj in range(4):
                                kc = kg * 4 + jj
                                dst = xfT.rearrange("p (kc t) -> p kc t", t=512)[
                                    :, kc, j * 128:(j + 1) * 128]
                                em.op("vector", lambda E, b=quad + jj, d=dst: E.tensor_copy(
                                    d, PSB(b, 128)), inc="sV")
                xf_done_V = em.cnt["sV"]

                em.wait("tensor", "sV", xf_done_V)
                for j in range(4):
                    for kc in range(KC):
                        em.op("tensor", lambda E, j=j, kc=kc: E.matmul(
                            PS(4 + j, NE),
                            xfT[:, kc * 512 + j * 128: kc * 512 + (j + 1) * 128],
                            wrt[:, kc * NE:(kc + 1) * NE],
                            start=(kc == 0), stop=(kc == KC - 1)), inc="sT",
                        fin=(kc == KC - 1))
                rl_mark = em.cnt["sT"]
                em.wait("vector", "sT", rl_mark)
                for j in range(4):
                    em.op("vector", lambda E, j=j: E.tensor_copy(r8a[:, :], PS(4 + j, NE)), inc="sV")
                    em.op("vector", lambda E: E.reduce_max(m1v[:, :], r8a[:, :], axis=AX), inc="sV")
                    em.op("vector", lambda E: E.tensor_single_scalar(
                        r8b[:, :], r8a[:, :], m1v[:, :], ALU.is_equal), inc="sV")
                    em.op("vector", lambda E: E.scalar_tensor_tensor(
                        r8c[:, :], r8b[:, :], -1e9, r8a[:, :], ALU.mult, ALU.add), inc="sV")
                    em.op("vector", lambda E: E.reduce_max(m2v[:, :], r8c[:, :], axis=AX), inc="sV")
                    em.op("vector", lambda E: E.tensor_sub(sdv[:, :], m2v[:, :], m1v[:, :]), inc="sV")
                    em.wait("scalar", "sV")
                    em.op("scalar", lambda E: E.activation(out=g2v[:, :], in_=sdv[:, :], func=AF.Sigmoid), inc="sA")
                    em.wait("vector", "sA")
                    em.op("vector", lambda E: E.tensor_scalar(
                        g1v[:, :], g2v[:, :], -1.0, 1.0, ALU.mult, ALU.add), inc="sV")
                    em.op("vector", lambda E: E.tensor_single_scalar(
                        r8a[:, :], r8c[:, :], m2v[:, :], ALU.is_equal), inc="sV")
                    em.op("vector", lambda E: E.tensor_scalar_mul(r8b[:, :], r8b[:, :], g1v[:, :]), inc="sV")
                    em.op("vector", lambda E: E.tensor_scalar_mul(r8a[:, :], r8a[:, :], g2v[:, :]), inc="sV")
                    em.op("vector", lambda E: E.tensor_add(r8b[:, :], r8b[:, :], r8a[:, :]), inc="sV")
                    em.op("vector", lambda E: E.tensor_copy(w8b[:, :], r8b[:, :]), inc="sV")
                    em.wait("tensor", "sV")
                    em.op("tensor", lambda E, j=j: E.transpose(
                        PSB(4 + j, 128)[0:NE, :], w8b[:, :], ident[:, :]), inc="sT", fin=True)
                wt_mark = em.cnt["sT"]
                with nc.sbuf_tensor(f"w8t{mc}", [NE, 512], BF16) as w8t:
                    em.wait("vector", "sT", wt_mark)
                    for j in range(4):
                        em.op("vector", lambda E, j=j: E.tensor_copy(
                            w8t[:, j * 128:(j + 1) * 128], PSB(4 + j, 128)[0:NE, :]), inc="sV")
                    em.wait("tensor", "sV")
                    em.op("tensor", lambda E: E.matmul(
                        PS(5), indm[:, :], w8t[:, :], start=True, stop=True), inc="sT", fin=True)
                    for kc in range(KC):
                        em.op("tensor", lambda E, kc=kc: E.matmul(
                            PS(6), at16[:, kc * 128:(kc + 1) * 128],
                            xfT[:, kc * 512:(kc + 1) * 512],
                            start=(kc == 0), stop=(kc == KC - 1)), inc="sT",
                        fin=(kc == KC - 1))
                    h_mark = em.cnt["sT"]
                    em.wait("vector", "sT", h_mark)
                    em.op("vector", lambda E: E.tensor_copy(wrepb[:, :], PS(5)), inc="sV")
                    em.op("vector", lambda E: E.tensor_mul(gt[:, :], PS(6), wrepb[:, :]), inc="sV")
                gt_mark = em.cnt["sV"]

                with nc.sbuf_tensor(f"wbsl{mc}", [128, 2, 16 * 512], BF16) as wbsl:
                    base_marks = {}
                    for ec in range(NE):
                        pb = (ec % 2) * 4
                        for half in range(2):
                            li = ec * 2 + half
                            par = li % 2
                            if li >= 2:
                                em.wait("scalar", "sT", base_marks[li - 2])
                            else:
                                em.wait("scalar", "sT")
                            km = half * 16
                            src = wbt16[km * 128:(km + 16) * 128, ec * 512:(ec + 1) * 512]
                            em.op("scalar", lambda E, s2=src, par=par: E.dma_start(
                                out=wbsl.rearrange("p b (kc e) -> p b kc e", e=512)[:, par, :, :],
                                in_=s2.rearrange("(kc p) e -> p kc e", p=128)), inc="sW", n=16)
                            em.wait("tensor", "sW")
                            if li == 0:
                                em.wait("tensor", "sV", gt_mark)
                                em.wait("tensor", "sG")
                            if half == 0:
                                em.wait("tensor", "sV")
                            for kc in range(16):
                                for j in range(4):
                                    em.op("tensor",
                                          lambda E, kc=kc, j=j, par=par, pb=pb, half=half: E.matmul(
                                              PS(pb + j),
                                              xfT[:, (half * 16 + kc) * 512 + j * 128:
                                                  (half * 16 + kc) * 512 + (j + 1) * 128],
                                              wbsl[:, par, kc * 512:(kc + 1) * 512],
                                              start=(half == 0 and kc == 0), stop=False), inc="sT")
                            base_marks[li] = em.cnt["sT"]
                        for j in range(4):
                            em.op("tensor", lambda E, j=j, ec=ec, pb=pb: E.matmul(
                                PS(pb + j), onesr[:, 0:128],
                                bb16[:, ec * 512:(ec + 1) * 512], start=False, stop=False), inc="sT")
                            em.op("tensor", lambda E, j=j, ec=ec, pb=pb: E.matmul(
                                PS(pb + j), gt[:, j * 128:(j + 1) * 128],
                                b3[:, ec * 512:(ec + 1) * 512], start=False, stop=True), inc="sT",
                            fin=(j == 3))
                        ec_mark = em.cnt["sT"]
                        em.wait("vector", "sT", ec_mark)
                        for j in range(4):
                            em.op("vector", lambda E, j=j, ec=ec, pb=pb: E.tensor_add(
                                x1[:, j * D + ec * 512: j * D + (ec + 1) * 512],
                                PS(pb + j),
                                x1[:, j * D + ec * 512: j * D + (ec + 1) * 512]), inc="sV")
                em.wait("sync", "sV")
                for j in range(4):
                    tt = mc * 4 + j
                    em.op("sync", lambda E, tt=tt, j=j: E.dma_start(
                        out=y_out[tt * 128:(tt + 1) * 128, :],
                        in_=x1.rearrange("p (j d) -> p j d", d=D)[:, j, :]), inc="sS", n=16)

        final_S = em.cnt["sS"]

        with nc.Block() as blk:
            @blk.sync
            def _(E):
                for f in em.ops["sync"]:
                    f(E)
                E.wait_ge(em.H["sS"], final_S)

            @blk.tensor
            def _(E):
                for f in em.ops["tensor"]:
                    f(E)

            @blk.vector
            def _(E):
                for f in em.ops["vector"]:
                    f(E)

            @blk.scalar
            def _(E):
                for f in em.ops["scalar"]:
                    f(E)

            @blk.gpsimd
            def _(E):
                for f in em.ops["gpsimd"]:
                    f(E)

    return nc


_CACHE = {}


def _consts():
    if "c" not in _CACHE:
        bf = ml_dtypes.bfloat16
        ident = np.eye(128, dtype=bf)
        masks = np.zeros((128, 4, 512), dtype=np.float32)
        n = np.arange(128)[:, None]
        m = np.arange(512)[None, :]
        for d in range(4):
            masks[:, d, :] = ((d * 128 + n) <= m).astype(np.float32)
        masks = masks.reshape(128, 4 * 512).astype(bf)
        indm = np.zeros((NE, 128), dtype=bf)
        for e in range(NE):
            indm[e, e * 16:(e + 1) * 16] = 1
        invf = 1.0 / 10000.0 ** (np.arange(0, ROT, 2, dtype=np.float64) / ROT)
        _CACHE["c"] = (ident, masks, indm, invf)
    return _CACHE["c"]


def kernel(L0_attn_hidden_states, L0_attn_Wq, L0_attn_Wk, L0_attn_Wv, L0_attn_Wo,
           L0_ffn_W_base, L0_ffn_b_base, L0_ffn_W_router, L0_ffn_A, L0_ffn_B):
    ident, masks, indm, invf = _consts()
    if "nc" not in _CACHE:
        _CACHE["nc"] = build_program()
    nc = _CACHE["nc"]

    f32 = np.float32
    x = np.ascontiguousarray(np.asarray(L0_attn_hidden_states, f32).reshape(S, D))
    wq = np.ascontiguousarray(np.asarray(L0_attn_Wq, f32))
    wk = np.ascontiguousarray(np.asarray(L0_attn_Wk, f32))
    wv = np.ascontiguousarray(np.asarray(L0_attn_Wv, f32))
    wo = np.ascontiguousarray(np.asarray(L0_attn_Wo, f32))
    wbt = np.ascontiguousarray(np.asarray(L0_ffn_W_base, f32).T)
    bb = np.ascontiguousarray(np.asarray(L0_ffn_b_base, f32).reshape(1, D))
    wrt_h = np.ascontiguousarray(np.asarray(L0_ffn_W_router, f32).T)
    amt = np.ascontiguousarray(np.asarray(L0_ffn_A, f32).reshape(128, D).T)
    bmr = np.ascontiguousarray(np.asarray(L0_ffn_B, f32).transpose(0, 2, 1).reshape(128, D))

    in_maps = []
    for c in range(NCORE):
        pos = np.arange(TL, dtype=np.float64) + c * TL
        ang = pos[None, :] * invf[:, None]
        cexp = np.repeat(np.cos(ang), 2, axis=0).astype(f32)
        sexp = np.repeat(np.sin(ang), 2, axis=0).astype(f32)
        in_maps.append({
            "x": x[c * TL:(c + 1) * TL],
            "wq": wq, "wk": wk, "wv": wv, "wo": wo, "wbt": wbt,
            "bb": bb, "wrt_h": wrt_h, "amt": amt, "bmr": bmr,
            "ident": ident, "masks": masks, "indm": indm,
            "cexp": cexp, "sexp": sexp,
        })
    res = run_bass_kernel_spmd(nc, in_maps, core_ids=list(range(NCORE)))
    y = np.concatenate([np.asarray(res.results[c]["y"]) for c in range(NCORE)], axis=0)
    return y.reshape(1, S, D).astype(f32)


# version 25
# speedup vs baseline: 1.0183x; 1.0039x over previous
import contextlib

import numpy as np
import ml_dtypes

import concourse.bass as bass
import concourse.mybir as mybir
from concourse.bass_utils import run_bass_kernel_spmd

F32 = mybir.dt.float32
BF16 = mybir.dt.bfloat16
AF = mybir.ActivationFunctionType
ALU = mybir.AluOpType
AX = mybir.AxisListType.X

NCORE = 8
D = 4096
S = 8192
TL = S // NCORE
NT = TL // 128
KC = D // 128
ECH = D // 128
HEADS = 16
HD = 256
ROT = 64
NE = 8
NR = 16
EPS = 1e-5

ENGS = ("sync", "tensor", "vector", "scalar", "gpsimd")


class Em:

    def __init__(self):
        self.ops = {e: [] for e in ENGS}
        self.cnt = {}
        self.waited = {e: {} for e in ENGS}
        self.H = {}

    def sems(self, *names):
        for n in names:
            self.cnt[n] = 0

    def op(self, eng, fn, inc=None, n=1, fin=False):
        if inc is not None:
            self.cnt[inc] += n
            if (eng in ("vector", "scalar") or fin) and n == 1:
                def g(E, f=fn, s=inc, k=n):
                    f(E)
                    E.maybe_drain_then_inc((self.H[s], k))
                self.ops[eng].append(g)
            else:
                self.ops[eng].append(
                    lambda E, f=fn, s=inc, k=n: f(E).then_inc(self.H[s], k))
            return self.cnt[inc]
        self.ops[eng].append(fn)
        return None

    def wait(self, eng, name, val=None):
        v = self.cnt[name] if val is None else val
        if v <= 0 or v <= self.waited[eng].get(name, 0):
            return
        self.waited[eng][name] = v
        self.ops[eng].append(lambda E, s=name, w=v: E.wait_ge(self.H[s], w))


def build_program():
    nc = bass.Bass("TRN2", target_bir_lowering=False, dynamic_dma_scratch_size=32768)

    x_in = nc.dram_tensor("x", [TL, D], F32, kind="ExternalInput").ap()
    wq_in = nc.dram_tensor("wq", [D, D], F32, kind="ExternalInput").ap()
    wk_in = nc.dram_tensor("wk", [D, D], F32, kind="ExternalInput").ap()
    wv_in = nc.dram_tensor("wv", [D, D], F32, kind="ExternalInput").ap()
    wo_in = nc.dram_tensor("wo", [D, D], F32, kind="ExternalInput").ap()
    wbt_in = nc.dram_tensor("wbt", [D, D], F32, kind="ExternalInput").ap()
    bb_in = nc.dram_tensor("bb", [1, D], F32, kind="ExternalInput").ap()
    wrt_in = nc.dram_tensor("wrt_h", [D, NE], F32, kind="ExternalInput").ap()
    amt_in = nc.dram_tensor("amt", [D, 128], F32, kind="ExternalInput").ap()
    bmr_in = nc.dram_tensor("bmr", [128, D], F32, kind="ExternalInput").ap()
    ident_in = nc.dram_tensor("ident", [128, 128], BF16, kind="ExternalInput").ap()
    masks_in = nc.dram_tensor("masks", [128, 4 * 512], BF16, kind="ExternalInput").ap()
    indm_in = nc.dram_tensor("indm", [NE, 128], BF16, kind="ExternalInput").ap()
    cexp_in = nc.dram_tensor("cexp", [ROT, TL], F32, kind="ExternalInput").ap()
    sexp_in = nc.dram_tensor("sexp", [ROT, TL], F32, kind="ExternalInput").ap()
    y_out = nc.dram_tensor("y", [TL, D], F32, kind="ExternalOutput").ap()

    qt_loc = nc.dram_tensor("qt_loc", [ECH, 128, TL], BF16, kind="Internal").ap()
    kt_loc = nc.dram_tensor("kt_loc", [ECH, 128, TL], BF16, kind="Internal").ap()
    v_loc = nc.dram_tensor("v_loc", [NT, 128, D], BF16, kind="Internal").ap()
    qt_all = nc.dram_tensor("qt_all", [NCORE, ECH, 128, TL], BF16, kind="Internal",
                            addr_space="Shared").ap()
    kt_all = nc.dram_tensor("kt_all", [NCORE, ECH, 128, TL], BF16, kind="Internal",
                            addr_space="Shared").ap()
    v_all = nc.dram_tensor("v_all", [NCORE, NT, 128, D], BF16, kind="Internal",
                           addr_space="Shared").ap()
    aot_loc = nc.dram_tensor("aot_loc", [4, 128, S], BF16, kind="Internal").ap()
    aot_all = nc.dram_tensor("aot_all", [NCORE, 4, 128, S], BF16, kind="Internal",
                             addr_space="Shared").ap()
    kt_my = nc.dram_tensor("kt_my", [4, 128, S], BF16, kind="Internal").ap()
    qt_my = nc.dram_tensor("qt_my", [4, 128, S], BF16, kind="Internal").ap()
    v_my = nc.dram_tensor("v_my", [NCORE, NT, 128, 512], BF16, kind="Internal").ap()
    aot_my = nc.dram_tensor("aot_my", [NCORE, 4, 128, TL], BF16, kind="Internal").ap()
    wbt16 = nc.dram_tensor("wbt16", [D, D], BF16, kind="Internal").ap()
    wo16 = nc.dram_tensor("wo16", [D, D], BF16, kind="Internal").ap()
    RG = [list(range(NCORE))]

    em = Em()
    em.sems("sS", "sG", "sW", "sT", "sV", "sA", "sC", "sCq", "sCk")
    PID = {}

    es = contextlib.ExitStack()
    with es:
        sb = lambda name, shape, dt: es.enter_context(nc.sbuf_tensor(name, shape, dt))
        ident = sb("identt", [128, 128], BF16)
        indm = sb("indmt", [NE, 128], BF16)
        at16 = sb("at16", [128, KC * 128], BF16)
        b3 = sb("b3", [128, KC * 128], BF16)
        wrt = sb("wrt", [128, KC * NE], BF16)
        bb16 = sb("bb16", [1, D], BF16)
        onesr = sb("onesr", [1, 128], BF16)
        stat = sb("stat", [128, 8, 6], F32)
        mvv = sb("mvv", [128, 2], F32)
        sdv = sb("sdv", [128, 1], F32)
        rsv = sb("rsv", [128, 1], F32)
        r8a = sb("r8a", [128, 8], F32)
        r8b = sb("r8b", [128, 8], F32)
        r8c = sb("r8c", [128, 8], F32)
        m1v = sb("m1v", [128, 1], F32)
        m2v = sb("m2v", [128, 1], F32)
        g2v = sb("g2v", [128, 1], F32)
        g1v = sb("g1v", [128, 1], F32)
        w8b = sb("w8b", [128, 8], BF16)
        gt = sb("gtb", [128, 512], BF16)
        wrepb = sb("wrepb", [128, 512], F32)
        epsv = sb("epsv", [128, 1], F32)

        psum = es.enter_context(nc.psum_tensor("ps", [128, 4096], F32))
        PS = lambda bank, w=512, off=0: psum[:, bank * 512 + off: bank * 512 + off + w]
        PSB = lambda bank, wb=128, offb=0: psum[:, bank * 512 + offb // 2:
                                                bank * 512 + offb // 2 + wb // 2].bitcast(BF16)

        for s in ("sS", "sG", "sW", "sT", "sV", "sA", "sC", "sCq", "sCk"):
            em.H[s] = es.enter_context(nc.semaphore(s))

        em.op("sync", lambda E: E.dma_start(out=ident[:, :], in_=ident_in), inc="sS", n=16)
        em.op("sync", lambda E: E.dma_start(out=indm[:, :], in_=indm_in), inc="sS", n=16)

        em.op("gpsimd", lambda E: E.dma_start(out=bb16[:, :], in_=bb_in), inc="sG", n=16)
        for i in range(4):
            em.op("gpsimd", lambda E, i=i: E.dma_start(
                out=wbt16[i * 1024:(i + 1) * 1024, :], in_=wbt_in[i * 1024:(i + 1) * 1024, :]),
                inc="sG", n=16)
            em.op("gpsimd", lambda E, i=i: E.dma_start(
                out=wo16[i * 1024:(i + 1) * 1024, :], in_=wo_in[i * 1024:(i + 1) * 1024, :]),
                inc="sG", n=16)

        em.op("vector", lambda E: E.memset(onesr[:, :], 1.0), inc="sV")
        em.op("vector", lambda E: E.memset(epsv[:, :], EPS), inc="sV")

        with nc.sbuf_tensor("xlnT", [128, KC * TL], BF16) as xlnT, \
             nc.sbuf_tensor("cexpt", [ROT, TL], F32) as cexp, \
             nc.sbuf_tensor("sexpt", [ROT, TL], F32) as sexp:
            em.op("sync", lambda E: E.dma_start(out=cexp[:, :], in_=cexp_in), inc="sS", n=16)
            em.op("sync", lambda E: E.dma_start(out=sexp[:, :], in_=sexp_in), inc="sS", n=16)
            with nc.sbuf_tensor("xbuf", [128, 2, D], F32) as xbuf, \
                 nc.sbuf_tensor("xln", [128, D], BF16) as xln:
                with nc.sbuf_tensor("prep32", [128, KC * 128], F32) as prep32:
                    em.op("sync", lambda E: E.dma_start(out=prep32[:, :], in_=bmr_in), inc="sS", n=16)
                    em.wait("vector", "sS")
                    em.op("vector", lambda E: E.tensor_scalar(
                        b3[:, :], prep32[:, :], 2.0, None, ALU.mult), inc="sV")
                    b3_done = em.cnt["sV"]
                    em.wait("sync", "sV", b3_done)
                    em.op("sync", lambda E: E.dma_start(
                        out=prep32.rearrange("p (kc j) -> p kc j", j=128),
                        in_=amt_in.rearrange("(kc p) j -> p kc j", p=128)), inc="sS", n=16)
                    em.wait("vector", "sS")
                    em.op("vector", lambda E: E.tensor_copy(at16[:, :], prep32[:, :]), inc="sV")
                    at_done = em.cnt["sV"]
                    em.wait("sync", "sV", at_done)
                    em.op("sync", lambda E: E.dma_start(
                        out=prep32.rearrange("p (kc e) -> p kc e", e=NE)[:, 0:KC, :],
                        in_=wrt_in.rearrange("(kc p) e -> p kc e", p=128)), inc="sS", n=16)
                    em.wait("vector", "sS")
                    em.op("vector", lambda E: E.tensor_copy(wrt[:, :], prep32[:, 0:KC * NE]), inc="sV")


                for tt in range(NT):
                    par = tt % 2
                    if tt >= 2:
                        em.wait("sync", "sA")
                    em.op("sync", lambda E, tt=tt, p=par: E.dma_start(
                        out=xbuf[:, p, :], in_=x_in[tt * 128:(tt + 1) * 128, :]), inc="sS", n=16)
                    em.wait("vector", "sS")
                    for c in range(8):
                        em.op("vector", lambda E, p=par, c=c: E.bn_stats(
                            stat[:, c, :], xbuf[:, p, 512 * c:512 * (c + 1)]), inc="sV")
                    em.op("vector", lambda E: E.bn_aggr(mvv[:, :], stat[:, :, :]), inc="sV")
                    em.wait("scalar", "sV")
                    em.op("scalar", lambda E: E.activation(out=sdv[:, :], in_=mvv[:, 1:2], func=AF.Sqrt,
                                                           bias=epsv[:, :]), inc="sA")
                    em.wait("vector", "sA")
                    em.op("vector", lambda E: E.reciprocal(rsv[:, :], sdv[:, :]), inc="sV")
                    em.op("vector", lambda E, p=par: E.tensor_scalar(
                        xln[:, :], xbuf[:, p, :], mvv[:, 0:1], rsv[:, :], ALU.subtract, ALU.mult), inc="sV")
                    em.wait("scalar", "sV")
                    em.op("scalar", lambda E: E.activation(out=sdv[:, :], in_=sdv[:, :], func=AF.Copy), inc="sA")
                    em.wait("tensor", "sV")
                    for kg in range(8):
                        quad = (kg % 2) * 4
                        em.wait("tensor", "sV")
                        for j in range(4):
                            kc = kg * 4 + j
                            em.op("tensor", lambda E, kc=kc, b=quad + j: E.transpose(
                                PSB(b, 128), xln[:, kc * 128:(kc + 1) * 128], ident[:, :]), inc="sT",
                                fin=(j == 3))
                        em.wait("vector", "sT")
                        for j in range(4):
                            kc = kg * 4 + j
                            dst = xlnT.rearrange("p (kc t) -> p kc t", t=TL)[
                                :, kc, tt * 128:(tt + 1) * 128]
                            em.op("vector", lambda E, b=quad + j, d=dst: E.tensor_copy(
                                d, PSB(b, 128)), inc="sV")

            def qk_phase(w_src, dst_loc, tag):
                with nc.sbuf_tensor("wsl" + tag, [128, 2, KC * 256], BF16) as wsl, \
                     nc.sbuf_tensor("wslf" + tag, [128, 2, KC * 128], F32) as wslf, \
                     nc.sbuf_tensor("stgf" + tag, [128, TL], F32) as stgf, \
                     nc.sbuf_tensor("stg16" + tag, [128, TL], BF16) as stg16, \
                     nc.sbuf_tensor("rot1" + tag, [ROT, 512], F32) as rot1, \
                     nc.sbuf_tensor("rot2" + tag, [ROT, 512], F32) as rot2, \
                     nc.sbuf_tensor("rotsw" + tag, [ROT, 512], F32) as rotsw:
                    slab_T = {}
                    cast_V = {}
                    em.wait("sync", "sT")
                    em.wait("sync", "sV")
                    em.wait("sync", "sA")
                    em.wait("sync", "sW")

                    def emit_half(s_, et):
                        hpar = (s_ * 2 + et) % 2
                        if (s_, et) >= (1, 0):
                            prev = (s_ * 2 + et) - 2
                            if prev >= 0:
                                em.wait("sync", "sV", cast_V.get(prev, 0))
                        src = w_src[:, s_ * 256 + et * 128: s_ * 256 + (et + 1) * 128].rearrange(
                            "(kc p) e -> p kc e", p=128)
                        em.op("sync", lambda E, src=src, hp=hpar: E.dma_start(
                            out=wslf.rearrange("p b (kc e) -> p b kc e", e=128)[:, hp, :, :],
                            in_=src), inc="sS", n=16)
                        ld = em.cnt["sS"]
                        em.wait("vector", "sS", ld)
                        if s_ >= 2:
                            em.wait("vector", "sT", slab_T[s_ - 2])
                        dst = wsl.rearrange("p b (kc e) -> p b kc e", e=256)[
                            :, s_ % 2, :, et * 128:(et + 1) * 128]
                        em.op("vector", lambda E, hp=hpar, d=dst: E.tensor_copy(
                            d, wslf.rearrange("p b (kc e) -> p b kc e", e=128)[:, hp, :, :]),
                            inc="sV")
                        cast_V[s_ * 2 + et] = em.cnt["sV"]

                    emit_half(0, 0)
                    emit_half(0, 1)
                    for s_ in range(16):
                        par = s_ % 2
                        if s_ + 1 < 16:
                            emit_half(s_ + 1, 0)
                            emit_half(s_ + 1, 1)
                        for et in range(2):
                            e_g = s_ * 2 + et
                            em.wait("tensor", "sV", cast_V[s_ * 2 + et])
                            for ch in range(2):
                                bank = et * 2 + ch
                                em.wait("tensor", "sA")
                                for kc in range(KC):
                                    em.op("tensor",
                                          lambda E, kc=kc, b=bank, p=par, et=et, ch=ch: E.matmul(
                                              PS(b),
                                              wsl[:, p, kc * 256 + et * 128: kc * 256 + (et + 1) * 128],
                                              xlnT[:, kc * TL + ch * 512: kc * TL + (ch + 1) * 512],
                                              start=(kc == 0), stop=(kc == KC - 1)), inc="sT",
                                          fin=(kc == KC - 1))
                            em.wait("scalar", "sT")
                            em.wait("scalar", "sV")
                            for ch in range(2):
                                bank = et * 2 + ch
                                em.op("scalar", lambda E, b=bank, ch=ch: E.activation(
                                    out=stgf[:, ch * 512:(ch + 1) * 512], in_=PS(b), func=AF.Copy),
                                    inc="sA")
                            rot = (e_g % 2 == 0)
                            if rot:
                                em.wait("vector", "sA")
                                for ch in range(2):
                                    sl = slice(ch * 512, (ch + 1) * 512)
                                    em.op("vector", lambda E, sl=sl: E.stream_shuffle(
                                        rotsw[:, :], stgf[0:ROT, sl], [p ^ 1 for p in range(32)]), inc="sV")
                                    em.op("vector", lambda E, sl=sl: E.tensor_mul(
                                        rot1[:, :], stgf[0:ROT, sl], cexp[:, sl]), inc="sV")
                                    em.op("vector", lambda E, sl=sl: E.tensor_mul(
                                        rot2[:, :], rotsw[:, :], sexp[:, sl]), inc="sV")
                                    em.op("vector", lambda E, sl=sl: E.tensor_add(
                                        stgf[0:ROT, sl], rot1[:, :], rot2[:, :]), inc="sV")
                            em.wait("vector", "sA")
                            em.wait("vector", "sW")
                            em.op("vector", lambda E: E.tensor_copy(stg16[:, :], stgf[:, :]), inc="sV")
                            em.wait("scalar", "sV")
                            em.op("scalar", lambda E, e_g=e_g, d=dst_loc: E.dma_start(
                                out=d[e_g, :, :], in_=stg16[:, :]), inc="sW", n=16)
                        slab_T[s_] = em.cnt["sT"]

            qk_phase(wq_in, qt_loc, "q")
            em.ops["gpsimd"].insert(0, lambda E: PID.__setitem__("g", E.partition_id()))
            qk_phase(wk_in, kt_loc, "k")

            with nc.sbuf_tensor("vsl", [128, 2, KC * 256], BF16) as vsl, \
                 nc.sbuf_tensor("vslf", [128, 2, KC * 128], F32) as vslf, \
                 nc.sbuf_tensor("vstg", [128, 2, 256], BF16) as vstg:
                slab_T = {}
                cast_V = {}
                em.wait("sync", "sT")
                em.wait("sync", "sV")
                em.wait("sync", "sA")
                em.wait("sync", "sW")

                def emit_vhalf(s_, et):
                    hpar = (s_ * 2 + et) % 2
                    prev = (s_ * 2 + et) - 2
                    if prev >= 0:
                        em.wait("sync", "sV", cast_V.get(prev, 0))
                    src = wv_in[:, s_ * 256 + et * 128: s_ * 256 + (et + 1) * 128].rearrange(
                        "(kc p) e -> p kc e", p=128)
                    em.op("sync", lambda E, src=src, hp=hpar: E.dma_start(
                        out=vslf.rearrange("p b (kc e) -> p b kc e", e=128)[:, hp, :, :],
                        in_=src), inc="sS", n=16)
                    ld = em.cnt["sS"]
                    em.wait("vector", "sS", ld)
                    if s_ >= 2:
                        em.wait("vector", "sT", slab_T[s_ - 2])
                    dst = vsl.rearrange("p b (kc e) -> p b kc e", e=256)[
                        :, s_ % 2, :, et * 128:(et + 1) * 128]
                    em.op("vector", lambda E, hp=hpar, d=dst: E.tensor_copy(
                        d, vslf.rearrange("p b (kc e) -> p b kc e", e=128)[:, hp, :, :]),
                        inc="sV")
                    cast_V[s_ * 2 + et] = em.cnt["sV"]

                emit_vhalf(0, 0)
                emit_vhalf(0, 1)
                for s_ in range(16):
                    par = s_ % 2
                    if s_ + 1 < 16:
                        emit_vhalf(s_ + 1, 0)
                        emit_vhalf(s_ + 1, 1)
                    em.wait("tensor", "sV", cast_V[s_ * 2 + 1])
                    em.wait("tensor", "sA")
                    for tt in range(NT):
                        for kc in range(KC):
                            em.op("tensor", lambda E, kc=kc, tt=tt, p=par: E.matmul(
                                PS(tt, 256), xlnT[:, kc * TL + tt * 128: kc * TL + (tt + 1) * 128],
                                vsl[:, p, kc * 256:(kc + 1) * 256],
                                start=(kc == 0), stop=(kc == KC - 1)), inc="sT",
                            fin=(kc == KC - 1))
                    em.wait("scalar", "sT")
                    for tt in range(NT):
                        vp = tt % 2
                        if tt >= 2:
                            em.wait("scalar", "sW")
                        em.op("scalar", lambda E, tt=tt, vp=vp: E.activation(
                            out=vstg[:, vp, :], in_=PS(tt, 256), func=AF.Copy), inc="sA")
                        em.op("scalar", lambda E, tt=tt, s=s_, vp=vp: E.dma_start(
                            out=v_loc[tt, :, s * 256:(s + 1) * 256], in_=vstg[:, vp, :]),
                            inc="sW", n=16)
                    slab_T[s_] = em.cnt["sT"]

        em.wait("gpsimd", "sS")
        em.wait("gpsimd", "sW")
        em.op("gpsimd", lambda E: E.collective_compute(
            "AllGather", ALU.bypass, ins=[qt_loc], outs=[qt_all], replica_groups=RG), inc="sCq")
        em.op("gpsimd", lambda E: E.collective_compute(
            "AllGather", ALU.bypass, ins=[kt_loc], outs=[kt_all], replica_groups=RG), inc="sCk")
        em.op("gpsimd", lambda E: E.collective_compute(
            "AllGather", ALU.bypass, ins=[v_loc], outs=[v_all], replica_groups=RG), inc="sC")
        coll1 = em.cnt["sC"]
        em.wait("gpsimd", "sCq")
        pre_repack = em.cnt["sG"]
        def rq(E):
            pid = PID["g"]
            return E.dma_start(
                out=qt_my.rearrange("c p (r t) -> c p r t", t=TL),
                in_=qt_all[:, bass.ts(pid, 4), :, :].rearrange("r c p t -> c p r t"))
        em.op("gpsimd", rq, inc="sCq", n=16)
        em.wait("gpsimd", "sCk")
        def rk(E):
            pid = PID["g"]
            return E.dma_start(
                out=kt_my.rearrange("c p (r t) -> c p r t", t=TL),
                in_=kt_all[:, bass.ts(pid, 4), :, :].rearrange("r c p t -> c p r t"))
        em.op("gpsimd", rk, inc="sCk", n=16)
        em.wait("gpsimd", "sC", coll1)
        def rv(E):
            pid = PID["g"]
            return E.dma_start(out=v_my[:, :, :, :], in_=v_all[:, :, :, bass.ts(pid, 512)])
        em.op("gpsimd", rv, inc="sG", n=16)
        rv_done = em.cnt["sG"]

        with nc.sbuf_tensor("aosb", [128, 4 * S], BF16) as aosb, \
             nc.sbuf_tensor("masksb", [128, 4 * 512], BF16) as masks, \
             nc.sbuf_tensor("qbuf", [128, 2, 2 * 512], BF16) as qbuf, \
             nc.sbuf_tensor("ktbuf", [128, 2, 2 * 512], BF16) as ktbuf, \
             nc.sbuf_tensor("vbuf", [128, 2, 4 * 257], BF16) as vbuf, \
             nc.sbuf_tensor("ptbuf", [128, 4, 512], BF16) as ptbuf, \
             nc.sbuf_tensor("otmp", [128, 4 * 256], BF16) as otmp, \
             nc.sbuf_tensor("rcp", [128, 4], F32) as rcp:
            em.op("sync", lambda E: E.dma_start(out=masks[:, :], in_=masks_in), inc="sS", n=16)
            em.wait("vector", "sV")
            em.wait("vector", "sG", pre_repack)
            em.wait("vector", "sT")
            em.wait("vector", "sA")
            em.wait("vector", "sS")
            em.wait("vector", "sW")
            for p in range(2):
                for nt in range(4):
                    em.op("vector", lambda E, p=p, nt=nt: E.memset(
                        vbuf[:, p, nt * 257 + 256: nt * 257 + 257], 1.0), inc="sV")
            vones_V = em.cnt["sV"]
            em.wait("sync", "sC", coll1)
            em.wait("tensor", "sC", coll1)
            em.wait("sync", "sCq")
            em.wait("sync", "sCk")
            em.wait("tensor", "sCq")
            em.wait("tensor", "sCk")
            em.wait("tensor", "sV", vones_V)
            em.wait("tensor", "sS")

            for hh in range(2):
                for qc in range(16):
                    qpar = qc % 2
                    em.wait("sync", "sT")
                    for c in range(2):
                        em.op("sync", lambda E, c=c, hh=hh, qc=qc, qp=qpar: E.dma_start(
                            out=qbuf[:, qp, c * 512:(c + 1) * 512],
                            in_=qt_my[2 * hh + c, :, qc * 512:(qc + 1) * 512]), inc="sS", n=16)
                    E_g = qc + 1
                    kv_kt = {}
                    kv_marks = {}
                    exp_marks = {}
                    av_marks = {}
                    logit_marks = {}
                    SB = (0, 1, 6, 7)
                    ntiles = 4 * E_g

                    def emit_kv(g):
                        gpar = g % 2
                        if g >= 2:
                            em.wait("sync", "sT", kv_marks[g - 2])
                        for c in range(2):
                            em.op("sync", lambda E, c=c, hh=hh, g=g, gp=gpar: E.dma_start(
                                out=ktbuf[:, gp, c * 512:(c + 1) * 512],
                                in_=kt_my[2 * hh + c, :, g * 512:(g + 1) * 512]), inc="sS", n=16)
                        kv_kt[g] = em.cnt["sS"]
                        em.wait("sync", "sG", rv_done)

                        def vd(E, hh=hh, g=g, gp=gpar):
                            src = v_my[g // 2, (g % 2) * 4:(g % 2) * 4 + 4, :,
                                       hh * 256:(hh + 1) * 256]
                            dst = vbuf.rearrange("p b (nt e) -> p b nt e", e=257)[:, gp, :, 0:256]
                            return E.dma_start(out=dst, in_=src.rearrange("nt p e -> p nt e"))
                        em.op("sync", vd, inc="sS", n=16)
                        return em.cnt["sS"]

                    kv_loads = {0: emit_kv(0)}
                    em.wait("tensor", "sV")

                    def emit_logits(idx, last):
                        g, nt = idx // 4, idx % 4
                        sp = SB[idx % 4]
                        gpar = g % 2
                        em.wait("tensor", "sS", kv_kt[g])
                        if idx >= 4:
                            em.wait("tensor", "sA", exp_marks[idx - 4])
                        for c in range(2):
                            em.op("tensor",
                                  lambda E, c=c, sp=sp, gp=gpar, nt=nt, qp=qpar: E.matmul(
                                      PS(sp),
                                      ktbuf[:, gp, c * 512 + nt * 128: c * 512 + nt * 128 + 128],
                                      qbuf[:, qp, c * 512:(c + 1) * 512],
                                      start=(c == 0), stop=(c == 1)), inc="sT",
                                  fin=(c == 1))
                        logit_marks[idx] = em.cnt["sT"]

                    emit_logits(0, ntiles == 1)
                    for idx in range(ntiles):
                        g, nt = idx // 4, idx % 4
                        sp = SB[idx % 4]
                        diag = (g == E_g - 1)
                        if nt == 0 and g + 1 < E_g and (g + 1) not in kv_loads:
                            kv_loads[g + 1] = emit_kv(g + 1)
                        if idx + 1 < ntiles:
                            emit_logits(idx + 1, idx + 1 == ntiles - 1)
                        em.wait("scalar", "sT", logit_marks[idx])
                        if idx >= 3:
                            k2 = idx - 3 if (idx - 3) % 2 == 1 else idx - 4
                            if k2 >= 0:
                                em.wait("scalar", "sT", av_marks[k2])
                        em.op("scalar", lambda E, sp=idx % 4, b=sp: E.activation(
                            out=ptbuf[:, sp, :], in_=PS(b), func=AF.Exp, scale=0.0625), inc="sA")
                        exp_marks[idx] = em.cnt["sA"]
                        if diag:
                            em.wait("vector", "sA", exp_marks[idx])
                            em.op("vector", lambda E, sp=idx % 4, nt=nt: E.tensor_mul(
                                ptbuf[:, sp, :], ptbuf[:, sp, :],
                                masks[:, nt * 512:(nt + 1) * 512]), inc="sV")
                            mask_mark = em.cnt["sV"]
                        em.wait("tensor", "sA", exp_marks[idx])
                        if diag:
                            em.wait("tensor", "sV", mask_mark)
                        em.wait("tensor", "sS", kv_loads[g])
                        for j in range(4):
                            em.op("tensor",
                                  lambda E, j=j, sp=idx % 4, gp=g % 2, g=g, nt=nt, qc=qc: E.matmul(
                                      PS(2 + j, 257), ptbuf[:, sp, j * 128:(j + 1) * 128],
                                      vbuf[:, gp, nt * 257: nt * 257 + 257],
                                      start=(g == 0 and nt == 0),
                                      stop=(g == qc and nt == 3)), inc="sT",
                                  fin=(j == 3 and (idx % 2 == 1 or idx == ntiles - 1)))
                        av_marks[idx] = em.cnt["sT"]
                        if nt == 3:
                            kv_marks[g] = em.cnt["sT"]
                    av_done = em.cnt["sT"]
                    em.wait("vector", "sT", av_done)
                    for j in range(4):
                        em.op("vector", lambda E, j=j: E.reciprocal(
                            rcp[:, j:j + 1], PS(2 + j, 1, 256)), inc="sV")
                    rec_mark = em.cnt["sV"]
                    em.wait("scalar", "sV", rec_mark)
                    for j in range(4):
                        em.op("scalar", lambda E, j=j: E.activation(
                            out=otmp[:, j * 256:(j + 1) * 256], in_=PS(2 + j, 256),
                            func=AF.Copy, scale=rcp[:, j:j + 1]), inc="sA")
                    norm_mark = em.cnt["sA"]
                    em.wait("tensor", "sA", norm_mark)
                    for j in range(4):
                        if j >= 1:
                            em.wait("tensor", "sV")
                        for c in range(2):
                            em.op("tensor", lambda E, j=j, c=c: E.transpose(
                                PSB(6 + c, 128),
                                otmp[:, j * 256 + c * 128: j * 256 + (c + 1) * 128],
                                ident[:, :]), inc="sT", fin=(c == 1))
                        tr_mark = em.cnt["sT"]
                        em.wait("vector", "sT", tr_mark)
                        for c in range(2):
                            em.op("vector", lambda E, j=j, c=c, hh=hh, qc=qc: E.tensor_copy(
                                aosb[:, (2 * hh + c) * S + qc * 512 + j * 128:
                                     (2 * hh + c) * S + qc * 512 + (j + 1) * 128],
                                PSB(6 + c, 128)), inc="sV")
                    em.wait("tensor", "sV")
                    em.op("tensor", lambda E: E.matmul(
                        PS(0, 128), ident[0:1, 0:128], ident[0:1, 0:128],
                        start=True, stop=True), inc="sT", fin=True)
                em.wait("sync", "sV")
                for c in range(2):
                    cc = 2 * hh + c
                    for r in range(NCORE):
                        em.op("sync", lambda E, cc=cc, r=r: E.dma_start(
                            out=aot_loc[cc, :, r * TL:(r + 1) * TL],
                            in_=aosb[:, cc * S + r * TL: cc * S + (r + 1) * TL]), inc="sS", n=16)
            aot_stores = em.cnt["sS"]

        em.wait("gpsimd", "sS", aot_stores)
        em.op("gpsimd", lambda E: E.collective_compute(
            "AllGather", ALU.bypass, ins=[aot_loc], outs=[aot_all], replica_groups=RG), inc="sC")
        coll2 = em.cnt["sC"]
        em.wait("gpsimd", "sC", coll2)
        def ra(E):
            pid = PID["g"]
            return E.dma_start(out=aot_my[:, :, :, :],
                               in_=aot_all[:, :, :, bass.ts(pid, TL)])
        em.op("gpsimd", ra, inc="sG", n=16)
        aot_rmark = {r: em.cnt["sG"] for r in range(NCORE)}

        with nc.sbuf_tensor("x1b", [128, 4 * D], F32) as x1, \
             nc.sbuf_tensor("xfT", [128, KC * 512], BF16) as xfT:
            em.wait("sync", "sC", coll2)
            for mc in range(2):
                if mc == 1:
                    em.wait("sync", "sS")
                for j in range(4):
                    tt = mc * 4 + j
                    em.op("sync", lambda E, tt=tt, j=j: E.dma_start(
                        out=x1.rearrange("p (j d) -> p j d", d=D)[:, j, :],
                        in_=x_in[tt * 128:(tt + 1) * 128, :]), inc="sS", n=16)
                with nc.sbuf_tensor(f"aosl{mc}", [128, KC * 512], BF16) as aosl, \
                     nc.sbuf_tensor(f"wosl{mc}", [128, 2, KC * 256], BF16) as wosl:
                    em.wait("sync", "sG", aot_rmark[NCORE - 1])
                    for dc in range(KC):
                        em.op("sync", lambda E, dc=dc, mc=mc: E.dma_start(
                            out=aosl[:, dc * 512:(dc + 1) * 512],
                            in_=aot_my[dc // 4, dc % 4, :, mc * 512:(mc + 1) * 512]), inc="sS", n=16)
                    x_ao_loaded = em.cnt["sS"]
                    wo_marks = {}
                    for ec in range(16):
                        par = ec % 2
                        if ec >= 2:
                            em.wait("scalar", "sT", wo_marks[ec - 2])
                        elif mc == 1:
                            em.wait("scalar", "sT")
                        for q in range(4):
                            src = wo16[:, ec * 256:(ec + 1) * 256].rearrange(
                                "(kc p) e -> p kc e", p=128)[:, q * 8:(q + 1) * 8, :]
                            dst = wosl.rearrange("p b (kc e) -> p b kc e", e=256)[
                                :, par, q * 8:(q + 1) * 8, :]
                            em.op("scalar", lambda E, s=src, d=dst: E.dma_start(out=d, in_=s),
                                  inc="sW", n=16)
                        em.wait("tensor", "sW")
                        em.wait("tensor", "sS", x_ao_loaded)
                        pb = (ec % 2) * 4
                        em.wait("tensor", "sV")
                        for kc in range(KC):
                            for j in range(4):
                                em.op("tensor", lambda E, kc=kc, j=j, p=par, pb=pb: E.matmul(
                                    PS(pb + j, 256),
                                    aosl[:, kc * 512 + j * 128: kc * 512 + (j + 1) * 128],
                                    wosl[:, p, kc * 256:(kc + 1) * 256],
                                    start=(kc == 0), stop=(kc == KC - 1)), inc="sT",
                                fin=(kc == KC - 1 and j == 3))
                        wo_marks[ec] = em.cnt["sT"]
                        em.wait("vector", "sT", wo_marks[ec])
                        for j in range(4):
                            em.op("vector", lambda E, j=j, ec=ec, pb=pb: E.tensor_add(
                                x1[:, j * D + ec * 256: j * D + (ec + 1) * 256],
                                PS(pb + j, 256),
                                x1[:, j * D + ec * 256: j * D + (ec + 1) * 256]), inc="sV")
                    x1_done_V = em.cnt["sV"]

                with nc.sbuf_tensor(f"xln2{mc}", [128, D], BF16) as xln2:
                    em.wait("vector", "sV", x1_done_V)
                    for j in range(4):
                        for c in range(8):
                            em.op("vector", lambda E, j=j, c=c: E.bn_stats(
                                stat[:, c, :], x1[:, j * D + 512 * c: j * D + 512 * (c + 1)]), inc="sV")
                        em.op("vector", lambda E: E.bn_aggr(mvv[:, :], stat[:, :, :]), inc="sV")
                        em.wait("scalar", "sV")
                        em.op("scalar", lambda E: E.activation(out=sdv[:, :], in_=mvv[:, 1:2], func=AF.Sqrt,
                                                               bias=epsv[:, :]), inc="sA")
                        em.wait("vector", "sA")
                        em.op("vector", lambda E: E.reciprocal(rsv[:, :], sdv[:, :]), inc="sV")
                        em.op("vector", lambda E, j=j: E.tensor_scalar(
                            xln2[:, :], x1[:, j * D:(j + 1) * D], mvv[:, 0:1], rsv[:, :],
                            ALU.subtract, ALU.mult), inc="sV")
                        em.wait("tensor", "sV")
                        for kg in range(8):
                            quad = (kg % 2) * 4
                            em.wait("tensor", "sV")
                            for jj in range(4):
                                kc = kg * 4 + jj
                                em.op("tensor", lambda E, kc=kc, b=quad + jj: E.transpose(
                                    PSB(b, 128), xln2[:, kc * 128:(kc + 1) * 128],
                                    ident[:, :]), inc="sT", fin=(jj == 3))
                            em.wait("vector", "sT")
                            for jj in range(4):
                                kc = kg * 4 + jj
                                dst = xfT.rearrange("p (kc t) -> p kc t", t=512)[
                                    :, kc, j * 128:(j + 1) * 128]
                                em.op("vector", lambda E, b=quad + jj, d=dst: E.tensor_copy(
                                    d, PSB(b, 128)), inc="sV")
                xf_done_V = em.cnt["sV"]

                em.wait("tensor", "sV", xf_done_V)
                for j in range(4):
                    for kc in range(KC):
                        em.op("tensor", lambda E, j=j, kc=kc: E.matmul(
                            PS(4 + j, NE),
                            xfT[:, kc * 512 + j * 128: kc * 512 + (j + 1) * 128],
                            wrt[:, kc * NE:(kc + 1) * NE],
                            start=(kc == 0), stop=(kc == KC - 1)), inc="sT",
                        fin=(kc == KC - 1))
                rl_mark = em.cnt["sT"]
                em.wait("vector", "sT", rl_mark)
                for j in range(4):
                    em.op("vector", lambda E, j=j: E.tensor_copy(r8a[:, :], PS(4 + j, NE)), inc="sV")
                    em.op("vector", lambda E: E.reduce_max(m1v[:, :], r8a[:, :], axis=AX), inc="sV")
                    em.op("vector", lambda E: E.tensor_single_scalar(
                        r8b[:, :], r8a[:, :], m1v[:, :], ALU.is_equal), inc="sV")
                    em.op("vector", lambda E: E.scalar_tensor_tensor(
                        r8c[:, :], r8b[:, :], -1e9, r8a[:, :], ALU.mult, ALU.add), inc="sV")
                    em.op("vector", lambda E: E.reduce_max(m2v[:, :], r8c[:, :], axis=AX), inc="sV")
                    em.op("vector", lambda E: E.tensor_sub(sdv[:, :], m2v[:, :], m1v[:, :]), inc="sV")
                    em.wait("scalar", "sV")
                    em.op("scalar", lambda E: E.activation(out=g2v[:, :], in_=sdv[:, :], func=AF.Sigmoid), inc="sA")
                    em.wait("vector", "sA")
                    em.op("vector", lambda E: E.tensor_scalar(
                        g1v[:, :], g2v[:, :], -1.0, 1.0, ALU.mult, ALU.add), inc="sV")
                    em.op("vector", lambda E: E.tensor_single_scalar(
                        r8a[:, :], r8c[:, :], m2v[:, :], ALU.is_equal), inc="sV")
                    em.op("vector", lambda E: E.tensor_scalar_mul(r8b[:, :], r8b[:, :], g1v[:, :]), inc="sV")
                    em.op("vector", lambda E: E.tensor_scalar_mul(r8a[:, :], r8a[:, :], g2v[:, :]), inc="sV")
                    em.op("vector", lambda E: E.tensor_add(r8b[:, :], r8b[:, :], r8a[:, :]), inc="sV")
                    em.op("vector", lambda E: E.tensor_copy(w8b[:, :], r8b[:, :]), inc="sV")
                    em.wait("tensor", "sV")
                    em.op("tensor", lambda E, j=j: E.transpose(
                        PSB(4 + j, 128)[0:NE, :], w8b[:, :], ident[:, :]), inc="sT", fin=True)
                wt_mark = em.cnt["sT"]
                with nc.sbuf_tensor(f"w8t{mc}", [NE, 512], BF16) as w8t:
                    em.wait("vector", "sT", wt_mark)
                    for j in range(4):
                        em.op("vector", lambda E, j=j: E.tensor_copy(
                            w8t[:, j * 128:(j + 1) * 128], PSB(4 + j, 128)[0:NE, :]), inc="sV")
                    em.wait("tensor", "sV")
                    em.op("tensor", lambda E: E.matmul(
                        PS(5), indm[:, :], w8t[:, :], start=True, stop=True), inc="sT", fin=True)
                    for kc in range(KC):
                        em.op("tensor", lambda E, kc=kc: E.matmul(
                            PS(6), at16[:, kc * 128:(kc + 1) * 128],
                            xfT[:, kc * 512:(kc + 1) * 512],
                            start=(kc == 0), stop=(kc == KC - 1)), inc="sT",
                        fin=(kc == KC - 1))
                    h_mark = em.cnt["sT"]
                    em.wait("vector", "sT", h_mark)
                    em.op("vector", lambda E: E.tensor_copy(wrepb[:, :], PS(5)), inc="sV")
                    em.op("vector", lambda E: E.tensor_mul(gt[:, :], PS(6), wrepb[:, :]), inc="sV")
                gt_mark = em.cnt["sV"]

                with nc.sbuf_tensor(f"wbsl{mc}", [128, 2, 16 * 512], BF16) as wbsl:
                    base_marks = {}
                    for ec in range(NE):
                        pb = (ec % 2) * 4
                        for half in range(2):
                            li = ec * 2 + half
                            par = li % 2
                            if li >= 2:
                                em.wait("scalar", "sT", base_marks[li - 2])
                            else:
                                em.wait("scalar", "sT")
                            km = half * 16
                            src = wbt16[km * 128:(km + 16) * 128, ec * 512:(ec + 1) * 512]
                            em.op("scalar", lambda E, s2=src, par=par: E.dma_start(
                                out=wbsl.rearrange("p b (kc e) -> p b kc e", e=512)[:, par, :, :],
                                in_=s2.rearrange("(kc p) e -> p kc e", p=128)), inc="sW", n=16)
                            em.wait("tensor", "sW")
                            if li == 0:
                                em.wait("tensor", "sV", gt_mark)
                                em.wait("tensor", "sG")
                            if half == 0:
                                em.wait("tensor", "sV")
                            for kc in range(16):
                                for j in range(4):
                                    em.op("tensor",
                                          lambda E, kc=kc, j=j, par=par, pb=pb, half=half: E.matmul(
                                              PS(pb + j),
                                              xfT[:, (half * 16 + kc) * 512 + j * 128:
                                                  (half * 16 + kc) * 512 + (j + 1) * 128],
                                              wbsl[:, par, kc * 512:(kc + 1) * 512],
                                              start=(half == 0 and kc == 0), stop=False), inc="sT")
                            base_marks[li] = em.cnt["sT"]
                        for j in range(4):
                            em.op("tensor", lambda E, j=j, ec=ec, pb=pb: E.matmul(
                                PS(pb + j), onesr[:, 0:128],
                                bb16[:, ec * 512:(ec + 1) * 512], start=False, stop=False), inc="sT")
                            em.op("tensor", lambda E, j=j, ec=ec, pb=pb: E.matmul(
                                PS(pb + j), gt[:, j * 128:(j + 1) * 128],
                                b3[:, ec * 512:(ec + 1) * 512], start=False, stop=True), inc="sT",
                            fin=(j == 3))
                        ec_mark = em.cnt["sT"]
                        em.wait("vector", "sT", ec_mark)
                        for j in range(4):
                            em.op("vector", lambda E, j=j, ec=ec, pb=pb: E.tensor_add(
                                x1[:, j * D + ec * 512: j * D + (ec + 1) * 512],
                                PS(pb + j),
                                x1[:, j * D + ec * 512: j * D + (ec + 1) * 512]), inc="sV")
                em.wait("sync", "sV")
                for j in range(4):
                    tt = mc * 4 + j
                    em.op("sync", lambda E, tt=tt, j=j: E.dma_start(
                        out=y_out[tt * 128:(tt + 1) * 128, :],
                        in_=x1.rearrange("p (j d) -> p j d", d=D)[:, j, :]), inc="sS", n=16)

        final_S = em.cnt["sS"]

        with nc.Block() as blk:
            @blk.sync
            def _(E):
                for f in em.ops["sync"]:
                    f(E)
                E.wait_ge(em.H["sS"], final_S)

            @blk.tensor
            def _(E):
                for f in em.ops["tensor"]:
                    f(E)

            @blk.vector
            def _(E):
                for f in em.ops["vector"]:
                    f(E)

            @blk.scalar
            def _(E):
                for f in em.ops["scalar"]:
                    f(E)

            @blk.gpsimd
            def _(E):
                for f in em.ops["gpsimd"]:
                    f(E)

    return nc


_CACHE = {}


def _consts():
    if "c" not in _CACHE:
        bf = ml_dtypes.bfloat16
        ident = np.eye(128, dtype=bf)
        masks = np.zeros((128, 4, 512), dtype=np.float32)
        n = np.arange(128)[:, None]
        m = np.arange(512)[None, :]
        for d in range(4):
            masks[:, d, :] = ((d * 128 + n) <= m).astype(np.float32)
        masks = masks.reshape(128, 4 * 512).astype(bf)
        indm = np.zeros((NE, 128), dtype=bf)
        for e in range(NE):
            indm[e, e * 16:(e + 1) * 16] = 1
        invf = 1.0 / 10000.0 ** (np.arange(0, ROT, 2, dtype=np.float64) / ROT)
        _CACHE["c"] = (ident, masks, indm, invf)
    return _CACHE["c"]


def kernel(L0_attn_hidden_states, L0_attn_Wq, L0_attn_Wk, L0_attn_Wv, L0_attn_Wo,
           L0_ffn_W_base, L0_ffn_b_base, L0_ffn_W_router, L0_ffn_A, L0_ffn_B):
    ident, masks, indm, invf = _consts()
    if "nc" not in _CACHE:
        _CACHE["nc"] = build_program()
    nc = _CACHE["nc"]

    f32 = np.float32
    x = np.ascontiguousarray(np.asarray(L0_attn_hidden_states, f32).reshape(S, D))
    wq = np.ascontiguousarray(np.asarray(L0_attn_Wq, f32))
    wk = np.ascontiguousarray(np.asarray(L0_attn_Wk, f32))
    wv = np.ascontiguousarray(np.asarray(L0_attn_Wv, f32))
    wo = np.ascontiguousarray(np.asarray(L0_attn_Wo, f32))
    wbt = np.ascontiguousarray(np.asarray(L0_ffn_W_base, f32).T)
    bb = np.ascontiguousarray(np.asarray(L0_ffn_b_base, f32).reshape(1, D))
    wrt_h = np.ascontiguousarray(np.asarray(L0_ffn_W_router, f32).T)
    amt = np.ascontiguousarray(np.asarray(L0_ffn_A, f32).reshape(128, D).T)
    bmr = np.ascontiguousarray(np.asarray(L0_ffn_B, f32).transpose(0, 2, 1).reshape(128, D))

    in_maps = []
    for c in range(NCORE):
        pos = np.arange(TL, dtype=np.float64) + c * TL
        ang = pos[None, :] * invf[:, None]
        cexp = np.repeat(np.cos(ang), 2, axis=0).astype(f32)
        sexp = np.repeat(np.sin(ang), 2, axis=0).astype(f32)
        in_maps.append({
            "x": x[c * TL:(c + 1) * TL],
            "wq": wq, "wk": wk, "wv": wv, "wo": wo, "wbt": wbt,
            "bb": bb, "wrt_h": wrt_h, "amt": amt, "bmr": bmr,
            "ident": ident, "masks": masks, "indm": indm,
            "cexp": cexp, "sexp": sexp,
        })
    res = run_bass_kernel_spmd(nc, in_maps, core_ids=list(range(NCORE)))
    y = np.concatenate([np.asarray(res.results[c]["y"]) for c in range(NCORE)], axis=0)
    return y.reshape(1, S, D).astype(f32)
